# version 21
# baseline (speedup 1.0000x reference)
import contextlib
import numpy as np
import concourse.bass as bass
import concourse.mybir as mybir
from concourse.bass_utils import run_bass_kernel_spmd

F32 = mybir.dt.float32
BF16 = mybir.dt.bfloat16
AF = mybir.ActivationFunctionType
ALU = mybir.AluOpType
AX = mybir.AxisListType.X

BT, TT_, DM = 4, 4096, 4096
NH, NG, HD = 32, 8, 128
TQ = 2048
QT = 256
NQT = TQ // QT
MSHIFT = 30.0
SCL = float(1.0 / np.sqrt(HD))
EPS = 1e-5
ROPE_THETA = 10000.0

ENGS = ("sync", "tensor", "vector", "scalar", "gpsimd")

PHASES = ["A1", "A2", "A3", "B", "C1", "C2"]
DEBUG_TAPS = []


class Builder:
    def __init__(self, nc, es):
        self.nc = nc
        self.es = es
        self.ops = {e: [] for e in ENGS}
        self.sems = {}
        self.cnt = {}
        self.lastw = {}

    def sem(self, name):
        if name not in self.sems:
            self.sems[name] = self.es.enter_context(self.nc.semaphore(name))
            self.cnt[name] = 0
        return self.sems[name]

    def bump(self, name, n=1):
        self.sem(name)
        self.cnt[name] += n
        return self.cnt[name]

    def val(self, name):
        self.sem(name)
        return self.cnt[name]

    def on(self, eng, fn):
        self.ops[eng].append(fn)

    def wait(self, eng, sname, target):
        if target <= 0:
            return
        if self.lastw.get((eng, sname), -1) >= target:
            return
        self.lastw[(eng, sname)] = target
        h = self.sem(sname)
        self.on(eng, lambda e, h=h, t=target: e.wait_ge(h, t))

    def waits(self, eng, conds):
        for sname, tgt in conds:
            self.wait(eng, sname, tgt)

    def run(self, eng, fn, inc=None, incn=1):
        if inc is None:
            self.on(eng, fn)
            return None
        h = self.sem(inc)
        tgt = self.bump(inc, incn)
        self.on(eng, lambda e, fn=fn, h=h, n=incn: fn(e).then_inc(h, n))
        return tgt

    def dma(self, queue, fn, dsem):
        h = self.sem(dsem)
        tgt = self.bump(dsem, 16)
        self.on(queue, lambda e, fn=fn, h=h: fn(e).then_inc(h, 16))
        return tgt

    def fence_loads(self, queue):
        for sname in ("pe", "ve", "ac"):
            if sname in self.cnt:
                self.wait(queue, sname, self.cnt[sname])
        self.wait_group(queue, "s_")

    def wait_group(self, eng, prefix):
        for sname in list(self.cnt):
            if sname.startswith(prefix):
                self.wait(eng, sname, self.cnt[sname])

    def barrier(self, tag=""):
        h = self.sem("ph")
        self.cnt["ph"] += 5
        tgt = self.cnt["ph"]
        for eng in ENGS:
            if eng == "sync":
                self.on(eng, lambda e, h=h: e.sem_inc(h, 1))
            else:
                self.on(eng, lambda e, h=h: e.drain().then_inc(h, 1))
            self.on(eng, lambda e, h=h, t=tgt: e.wait_ge(h, t))


class Slots:
    def __init__(self, b, name, n, view):
        self.b, self.name, self.n, self.view = b, name, n, view
        self.ready = [[] for _ in range(n)]
        self.free = [[] for _ in range(n)]
        self.i = 0

    def next(self):
        s = self.i % self.n
        self.i += 1
        return s

    def load(self, s, fn, queue="sync"):
        b = self.b
        b.waits(queue, self.free[s])
        self.free[s] = []
        dsem = f"d_{self.name}{s}"
        tgt = b.dma(queue, lambda e, fn=fn, d=self.view(s): fn(e, d), dsem)
        self.ready[s] = [(dsem, tgt)]
        return tgt

    def produce(self, s, conds):
        self.free[s] = []
        self.ready[s] = list(conds)

    def wait_ready(self, eng, s):
        self.b.waits(eng, self.ready[s])

    def consumed(self, s, sname, tgt):
        self.free[s].append((sname, tgt))

    def store(self, s, fn, queue="sync"):
        b = self.b
        dsem = f"s_{self.name}{s}"
        tgt = b.dma(queue, fn, dsem)
        self.consumed(s, dsem, tgt)
        return dsem, tgt


def _perm_vec():
    return np.concatenate([np.arange(0, HD, 2), np.arange(1, HD, 2)])


def _trig(positions):
    j = np.arange(HD // 2)
    freqs = 1.0 / ROPE_THETA ** (2.0 * j / HD)
    ang = positions.astype(np.float64)[None, :] * freqs[:, None].astype(np.float64)
    cos = np.cos(ang).astype(np.float32)
    sin = np.sin(ang).astype(np.float32)
    return (np.concatenate([cos, cos], 0), np.concatenate([-sin, sin], 0))


def _masks(j):
    out = np.zeros((NQT, 4, 128, QT), np.float32)
    t_id = np.arange(QT)[None, :]
    s_id = np.arange(128)[:, None]
    for i in range(NQT):
        g = 2 * i + j
        for r in range(4):
            c = 4 * i + r
            out[i, r] = (128 * c + s_id <= QT * g + t_id)
    return out.reshape(NQT * 4 * 128, QT)


def build_nc():
    nc = bass.Bass("TRN2", target_bir_lowering=False)
    es = contextlib.ExitStack()
    nc._es = es

    d = {}
    def din(name, shape, dtype=F32):
        d[name] = nc.dram_tensor(name, list(shape), dtype, kind="ExternalInput").ap()
        return d[name]
    def dscr(name, shape, dtype):
        d[name] = nc.dram_tensor(name, list(shape), dtype, kind="Internal").ap()
        return d[name]

    xkv = din("xkv", (TT_, DM))
    xq = din("xq", (TQ, DM))
    wq = din("wq", (DM, NH * HD))
    wk = din("wk", (DM, NG * HD))
    wv = din("wv", (DM, NG * HD))
    wo = din("wo", (NH * HD, DM))
    bo = din("bo", (1, DM))
    wf = din("wf", (DM, DM))
    fb = din("fb", (1, DM))
    lA = din("lA", (128, DM))
    lB = din("lB", (DM, 128))
    lw = din("lw", (128, 1))
    tqa = din("tqa", (128, TQ)); tqb = din("tqb", (128, TQ))
    tka = din("tka", (128, TT_)); tkb = din("tkb", (128, TT_))
    msk = din("msk", (NQT * 4 * 128, QT))
    iden = din("iden", (128, 128))
    swp = din("swp", (128, 128))
    out = nc.dram_tensor("out", [TQ, DM], F32, kind="ExternalOutput").ap()

    xn_kv = dscr("xn_kv", (TT_, DM), BF16)
    xn_q = dscr("xn_q", (TQ, DM), BF16)
    Kfm = dscr("Kfm", (NG * HD, TT_), BF16)
    Vtm = dscr("Vtm", (TT_, NG * HD), BF16)
    Qfm = dscr("Qfm", (NH * HD, TQ), BF16)
    Otm = dscr("Otm", (TQ, NH * HD), BF16)
    Ysc = dscr("Ysc", (TQ, DM), F32)
    YNtm = dscr("YNtm", (TQ, DM), BF16)
    Wt = dscr("Wt", (DM, DM), BF16)

    b = Builder(nc, es)
    sbt = lambda name, shape, dtype=BF16, **kw: es.enter_context(
        nc.sbuf_tensor(name, list(shape), dtype, **kw))

    bigA = sbt("bigA", (128, 65536), BF16, is_transpose_dest=True)
    ones1 = sbt("ones1", (1, 128), BF16)
    eps_col = sbt("eps_col", (128, 1), F32)
    negm_col = sbt("negm_col", (128, 1), F32)
    statc = sbt("statc", (128, 16), F32)
    ln2mu = sbt("ln2mu", (128, 16), F32)
    ln2rs = sbt("ln2rs", (128, 16), F32)
    s1all = sbt("s1all", (128, 256), F32)
    s2all = sbt("s2all", (128, 256), F32)
    h_fm = sbt("h_fm", (128, TQ), BF16)
    swapm = sbt("swapm", (128, 128), BF16)
    afm = sbt("afm", (128, 32 * 128), BF16)
    bblk = sbt("bblk", (128, DM), BF16)
    coefR = sbt("coefR", (128, 1), F32)
    ps = es.enter_context(nc.psum_tensor("ps", [128, 3584], F32))
    psT = es.enter_context(nc.psum_tensor("psT", [128, 128], BF16))
    bank = lambda i, n=512: ps[:, 512 * i:512 * i + n]
    bview = lambda off, ln: bigA[:, off:off + ln]

    b.run("vector", lambda e: e.memset(ones1[:], 1.0))
    b.run("vector", lambda e: e.memset(eps_col[:], EPS))
    b.run("vector", lambda e: e.memset(negm_col[:], -MSHIFT), inc="ve")
    b.dma("gpsimd", lambda e: e.dma_start(out=swapm[:], in_=swp), "d_c")
    for i in range(8):
        b.dma("gpsimd", lambda e, i=i: e.dma_start(
            out=Wt[512 * i:512 * (i + 1), :], in_=wf[512 * i:512 * (i + 1), :]), "d_wt")
    b.barrier("init")

    if "A1" in PHASES:
        with contextlib.ExitStack() as phs:
            xt = phs.enter_context(nc.sbuf_tensor("a1_xt", [128, 2 * DM], F32))
            junk = phs.enter_context(nc.sbuf_tensor("a1_junk", [128, DM], BF16))
            xno = phs.enter_context(nc.sbuf_tensor("a1_xno", [128, 2 * DM], BF16))
            xsl = Slots(b, "a1x", 2, lambda s: xt[:, s * DM:(s + 1) * DM])
            osl = Slots(b, "a1o", 2, lambda s: xno[:, s * DM:(s + 1) * DM])
            tiles = [(xkv, xn_kv, i) for i in range(TT_ // 128)] + \
                    [(xq, xn_q, i) for i in range(TQ // 128)]
            po_hist = []

            def load_a1(n):
                src_, dst_, i_ = tiles[n]
                ss = xsl.next()
                xsl.load(ss, lambda e, dd, src2=src_, i2=i_: e.dma_start(
                    out=dd, in_=src2[128 * i2:128 * (i2 + 1), :]), queue="sync")
                return ss

            def stageA(n, slot):
                sc0 = 8 * (n % 2)
                xv = xt[:, slot * DM:(slot + 1) * DM]
                xsl.wait_ready("scalar", slot)
                ac_t = b.run("scalar", lambda e, xv=xv, sc0=sc0: e.activation(
                    junk[:, 0:DM], xv, AF.Square, accum_out=statc[:, sc0 + 1:sc0 + 2]),
                    inc="ac")
                xsl.wait_ready("vector", slot)
                b.run("vector", lambda e, xv=xv, sc0=sc0: e.reduce_sum(
                    statc[:, sc0:sc0 + 1], xv, axis=AX))
                b.run("vector", lambda e: e.drain())
                return ac_t

            def stageB(n, slot, ac_sq):
                sc0 = 8 * (n % 2)
                src_, dst_, i_ = tiles[n]
                xv = xt[:, slot * DM:(slot + 1) * DM]
                b.wait("vector", "ac", ac_sq)
                b.run("vector", lambda e, sc0=sc0: e.tensor_scalar(
                    statc[:, sc0 + 2:sc0 + 3], statc[:, sc0:sc0 + 1], 1.0 / DM,
                    None, op0=ALU.mult))
                b.run("vector", lambda e: e.drain())
                b.run("vector", lambda e, sc0=sc0: e.tensor_tensor(
                    statc[:, sc0 + 3:sc0 + 4], statc[:, sc0 + 2:sc0 + 3],
                    statc[:, sc0 + 2:sc0 + 3], op=ALU.mult))
                b.run("vector", lambda e: e.drain())
                ve_t = b.run("vector", lambda e, sc0=sc0: e.tensor_scalar(
                    statc[:, sc0 + 4:sc0 + 5], statc[:, sc0 + 1:sc0 + 2], 1.0 / DM,
                    statc[:, sc0 + 3:sc0 + 4], op0=ALU.mult, op1=ALU.subtract),
                    inc="ve")
                b.wait("scalar", "ve", ve_t)
                ac_t = b.run("scalar", lambda e, sc0=sc0: e.activation(
                    statc[:, sc0 + 5:sc0 + 6], statc[:, sc0 + 4:sc0 + 5], AF.Sqrt,
                    bias=eps_col[:]), inc="ac")
                b.wait("vector", "ac", ac_t)
                b.run("vector", lambda e, sc0=sc0: e.reciprocal(
                    statc[:, sc0 + 6:sc0 + 7], statc[:, sc0 + 5:sc0 + 6]))
                b.run("vector", lambda e: e.drain())
                ve_t = b.run("vector", lambda e, sc0=sc0: e.tensor_scalar(
                    statc[:, sc0 + 7:sc0 + 8], statc[:, sc0 + 2:sc0 + 3],
                    statc[:, sc0 + 6:sc0 + 7], -1.0,
                    op0=ALU.mult, op1=ALU.mult), inc="ve")
                b.run("vector", lambda e: e.drain())
                o = osl.next()
                b.wait("scalar", "ve", ve_t)
                b.waits("scalar", osl.free[o])
                osl.free[o] = []
                ov = xno[:, o * DM:(o + 1) * DM]
                po_t = b.run("scalar", lambda e, xv=xv, ov=ov, sc0=sc0: e.activation(
                    ov, xv, AF.Identity, bias=statc[:, sc0 + 7:sc0 + 8],
                    scale=statc[:, sc0 + 6:sc0 + 7]), inc="ac")
                xsl.consumed(slot, "ac", po_t)
                po_hist.append(po_t)
                b.wait("sync", "ac", po_t)
                osl.store(o, lambda e, dst=dst_, i=i_, ov=ov: e.dma_start(
                    out=dst[128 * i:128 * (i + 1), :], in_=ov))

            slot_of = {0: load_a1(0)}
            prev = None
            for n_ in range(len(tiles)):
                slot = slot_of.pop(n_)
                ac_sq = stageA(n_, slot)
                if prev is not None:
                    stageB(*prev)
                if n_ + 1 < len(tiles):
                    slot_of[n_ + 1] = load_a1(n_ + 1)
                prev = (n_, slot, ac_sq)
            stageB(*prev)
            b.wait_group("sync", "s_a1o")
        b.barrier("A1")

    def proj_fm(tag, phs, w_dram, n_oc, nrows, dst_fm, trig_sb, trig_off, abs0):
        wbuf = phs.enter_context(nc.sbuf_tensor(f"{tag}_wb", [128, 2 * 4096], BF16))
        ev = phs.enter_context(nc.sbuf_tensor(f"{tag}_ev", [128, 4 * 512], BF16))
        rt = phs.enter_context(nc.sbuf_tensor(f"{tag}_rt", [128, 2 * 512], BF16))
        ko = phs.enter_context(nc.sbuf_tensor(f"{tag}_ko", [128, 4 * 512], BF16))
        wsl = Slots(b, f"{tag}w", 2, lambda s: wbuf[:, s * 4096:(s + 1) * 4096])
        esl = Slots(b, f"{tag}e", 4, lambda s: ev[:, s * 512:(s + 1) * 512])
        ksl = Slots(b, f"{tag}k", 4, lambda s: ko[:, s * 512:(s + 1) * 512])
        ntw = nrows // 512
        span = trig_sb.shape[1] // 2
        bankfree = {0: [], 1: [], 2: []}
        swapfree = {3: [], 4: [], 5: []}
        nswap = [0]
        pend = []
        b.fence_loads("gpsimd")
        b.wait("tensor", "d_c", b.val("d_c"))

        def flush_rope(item):
            es_, evv, ac_t, oc, tw = item
            sbk = 3 + nswap[0] % 3
            nswap[0] += 1
            b.wait("tensor", "ac", ac_t)
            b.waits("tensor", swapfree[sbk])
            swapfree[sbk] = []
            pe_t2 = b.run("tensor", lambda e, sbk=sbk, evv=evv: e.matmul(
                bank(sbk), swapm[:], evv, start=True, stop=True), inc="pe")
            ks = ksl.next()
            b.wait("vector", "ac", ac_t)
            b.waits("vector", ksl.free[ks])
            ksl.free[ks] = []
            kov = ko[:, ks * 512:(ks + 1) * 512]
            toff = trig_off + 512 * tw
            b.run("vector", lambda e, evv=evv, toff=toff: e.tensor_tensor(
                rt[:, 0:512], evv, trig_sb[:, toff:toff + 512], op=ALU.mult))
            b.wait("vector", "pe", pe_t2)
            ve_s = b.run("vector", lambda e, sbk=sbk, toff=toff, span=span:
                         e.tensor_tensor(
                rt[:, 512:1024], bank(sbk),
                trig_sb[:, span + toff:span + toff + 512], op=ALU.mult), inc="ve")
            swapfree[sbk] = [("ve", ve_s)]
            b.run("vector", lambda e: e.drain())
            ve_t = b.run("vector", lambda e, kov=kov: e.tensor_tensor(
                kov, rt[:, 0:512], rt[:, 512:1024], op=ALU.add), inc="ve")
            b.run("vector", lambda e: e.drain())
            esl.consumed(es_, "ve", ve_t)
            esl.consumed(es_, "pe", pe_t2)
            b.wait("sync", "ve", ve_t)
            p0 = abs0 + 512 * tw
            ksl.store(ks, lambda e, oc=oc, p0=p0, kov=kov: e.dma_start(
                out=dst_fm[128 * oc:128 * (oc + 1), p0:p0 + 512], in_=kov))
        for oc in range(n_oc):
            s = oc % 2
            wsl.load(s, lambda e, dd, oc=oc: e.dma_start(
                out=dd.rearrange("p (k o) -> p k o", o=128),
                in_=w_dram.rearrange("(k p) o -> p k o", p=128)[:, :, 128 * oc:128 * (oc + 1)]),
                queue="gpsimd")
            wsl.wait_ready("tensor", s)
            pe_t = None
            for tw in range(ntw):
                bk = (oc * ntw + tw) % 3
                b.waits("tensor", bankfree[bk])
                bankfree[bk] = []
                for k in range(32):
                    fn = lambda e, s=s, k=k, tw=tw, bk=bk: e.matmul(
                        bank(bk), wbuf[:, s * 4096 + 128 * k:s * 4096 + 128 * (k + 1)],
                        bview(nrows * k + 512 * tw, 512),
                        start=(k == 0), stop=(k == 31))
                    if k < 31:
                        b.run("tensor", fn)
                    else:
                        pe_t = b.run("tensor", fn, inc="pe")
                es_ = esl.next()
                b.wait("scalar", "pe", pe_t)
                b.waits("scalar", esl.free[es_])
                esl.free[es_] = []
                evv = ev[:, es_ * 512:(es_ + 1) * 512]
                ac_t = b.run("scalar", lambda e, bk=bk, evv=evv: e.activation(
                    evv, bank(bk), AF.Copy), inc="ac")
                bankfree[bk] = [("ac", ac_t)]
                pend.append((es_, evv, ac_t, oc, tw))
                if len(pend) >= 2:
                    flush_rope(pend.pop(0))
            wsl.consumed(s, "pe", pe_t)
        while pend:
            flush_rope(pend.pop(0))

    if "A2" in PHASES:
        with contextlib.ExitStack() as phs:
            trig = phs.enter_context(nc.sbuf_tensor("a2_trig", [128, 2 * 2048], BF16))
            for w in range(2):
                if w > 0:
                    b.wait("sync", "pe", b.val("pe"))
                b.wait("gpsimd", "ve", b.val("ve"))
                b.dma("gpsimd", lambda e, w=w: e.dma_start(
                    out=trig[:, 0:2048], in_=tka[:, 2048 * w:2048 * (w + 1)]), "d_tr")
                b.dma("gpsimd", lambda e, w=w: e.dma_start(
                    out=trig[:, 2048:4096], in_=tkb[:, 2048 * w:2048 * (w + 1)]), "d_tr")
                b.wait("vector", "d_tr", b.val("d_tr"))
                for c in range(32):
                    b.dma("sync", lambda e, c=c, w=w: e.dma_start_transpose(
                        bview(2048 * c, 2048),
                        xn_kv[2048 * w:2048 * (w + 1), 128 * c:128 * (c + 1)]), "d_t")
                b.wait("tensor", "d_t", b.val("d_t"))
                with contextlib.ExitStack() as kps:
                    proj_fm(f"a2w{w}", kps, wk, NG, 2048, Kfm, trig,
                            0, 2048 * w)
                with contextlib.ExitStack() as vps:
                    vwb = vps.enter_context(
                        nc.sbuf_tensor(f"a2v{w}_w", [128, 32 * 512], BF16))
                    vev = vps.enter_context(
                        nc.sbuf_tensor(f"a2v{w}_e", [128, 4 * 512], BF16))
                    vsl = Slots(b, f"a2v{w}e", 4, lambda s: vev[:, s * 512:(s + 1) * 512])
                    vbank = {0: [("ac", b.val("ac"))], 1: [], 2: []}
                    for ow in range(2):
                        b.fence_loads("gpsimd")
                        dw = b.dma("gpsimd", lambda e, ow=ow: e.dma_start(
                            out=vwb.rearrange("p (k o) -> p k o", o=512),
                            in_=wv.rearrange("(k p) o -> p k o", p=128)
                            [:, :, 512 * ow:512 * (ow + 1)]), f"d_vw{w}")
                        b.wait("tensor", f"d_vw{w}", dw)
                        for tt in range(16):
                            bk = tt % 3
                            b.waits("tensor", vbank[bk])
                            vbank[bk] = []
                            for k in range(32):
                                fn = lambda e, k=k, tt=tt, bk=bk: e.matmul(
                                    bank(bk), bview(2048 * k + 128 * tt, 128),
                                    vwb[:, 512 * k:512 * (k + 1)],
                                    start=(k == 0), stop=(k == 31))
                                if k < 31:
                                    b.run("tensor", fn)
                                else:
                                    pe_t = b.run("tensor", fn, inc="pe")
                            vs = vsl.next()
                            b.wait("scalar", "pe", pe_t)
                            b.waits("scalar", vsl.free[vs])
                            vsl.free[vs] = []
                            vv = vev[:, vs * 512:(vs + 1) * 512]
                            ac_t = b.run("scalar", lambda e, bk=bk, vv=vv: e.activation(
                                vv, bank(bk), AF.Copy), inc="ac")
                            vbank[bk] = [("ac", ac_t)]
                            b.wait("sync", "ac", ac_t)
                            vsl.store(vs, lambda e, w=w, tt=tt, ow=ow, vv=vv:
                                      e.dma_start(
                                out=Vtm[2048 * w + 128 * tt:2048 * w + 128 * (tt + 1),
                                        512 * ow:512 * (ow + 1)], in_=vv))
            b.wait_group("sync", "s_a2")
        b.barrier("A2")

    if "A3" in PHASES:
        with contextlib.ExitStack() as phs:
            trigq = phs.enter_context(nc.sbuf_tensor("a3_trig", [128, 2 * TQ], BF16))
            b.dma("gpsimd", lambda e: e.dma_start(out=trigq[:, 0:TQ], in_=tqa), "d_tr")
            b.dma("gpsimd", lambda e: e.dma_start(out=trigq[:, TQ:2 * TQ], in_=tqb), "d_tr")
            for c in range(32):
                b.dma("sync", lambda e, c=c: e.dma_start_transpose(
                    bview(2048 * c, 2048), xn_q[:, 128 * c:128 * (c + 1)]), "d_t")
            b.wait("tensor", "d_t", b.val("d_t"))
            b.wait("vector", "d_tr", b.val("d_tr"))
            proj_fm("a3", phs, wq, NH, 2048, Qfm, trigq, 0, 0)
            b.wait_group("sync", "s_a3")
        b.barrier("A3")

    if "B" in PHASES:
        with contextlib.ExitStack() as phs:
            maskb = bigA[:, 8192:8192 + NQT * 1024]
            kg = phs.enter_context(nc.sbuf_tensor("b_kg", [128, 2 * TT_], BF16))
            vh = phs.enter_context(nc.sbuf_tensor("b_vh", [128, 2 * 32 * 129], BF16))
            qtl = phs.enter_context(nc.sbuf_tensor("b_qt", [128, 4 * QT], BF16))
            pbf = phs.enter_context(nc.sbuf_tensor("b_p", [128, 3 * 1024], BF16))
            oev = phs.enter_context(nc.sbuf_tensor("b_oev", [128, 2 * QT], BF16))
            recl = phs.enter_context(nc.sbuf_tensor("b_recl", [128, 4], F32))
            b.fence_loads("gpsimd")
            b.dma("gpsimd", lambda e: e.dma_start(
                out=maskb.rearrange("p (i r t) -> p i r t", i=NQT, r=4),
                in_=msk.rearrange("(i r p) t -> p i r t", p=128, r=4)), "d_m")
            b.run("vector", lambda e: e.memset(
                vh[:, 0:32 * 129].rearrange("p (c o) -> p c o", o=129)[:, :, 128:129],
                1.0))
            b.run("vector", lambda e: e.memset(
                vh[:, 32 * 129:64 * 129].rearrange("p (c o) -> p c o", o=129)
                [:, :, 128:129], 1.0), inc="ve")
            b.run("vector", lambda e: e.drain())
            b.wait("vector", "d_m", b.val("d_m"))
            vh_ready_ve = b.val("ve")

            kgs = Slots(b, "bkg", 2, lambda s: kg[:, s * TT_:(s + 1) * TT_])
            vhs = Slots(b, "bvh", 2, lambda s: vh[:, s * 32 * 129:(s + 1) * 32 * 129])
            qts = Slots(b, "bqt", 4, lambda s: qtl[:, s * QT:(s + 1) * QT])
            psl = Slots(b, "bp", 3, lambda s: pbf[:, s * 1024:(s + 1) * 1024])
            osl = Slots(b, "bo", 2, lambda s: oev[:, s * QT:(s + 1) * QT])
            s_sub = lambda p, u: ps[:, 1024 * p + 256 * u:1024 * p + 256 * (u + 1)]
            s_pair = lambda p: ps[:, 1024 * p:1024 * p + 1024]
            pairfree = [[], []]
            obankfree = [[], []]

            def load_group(g):
                s = g % 2
                kgs.load(s, lambda e, dd, g=g: e.dma_start(
                    out=dd, in_=Kfm[128 * g:128 * (g + 1), :]))
                t = vhs.load(s, lambda e, dd, g=g: e.dma_start(
                    out=dd.rearrange("p (c o) -> p c o", o=129)[:, :, 0:128],
                    in_=Vtm.rearrange("(c p) d -> p c d", p=128)
                    [:, :, 128 * g:128 * (g + 1)]))
                vhs.ready[s].append(("ve", vh_ready_ve))

            if True:
                idn = bigA[:, 4096:4224]
                stag = bigA[:, 0:4096]
                b.dma("gpsimd", lambda e: e.dma_start(out=idn, in_=iden), "d_l")
                b.dma("gpsimd", lambda e: e.dma_start(out=coefR[:], in_=lw), "d_l")
                dl = b.dma("gpsimd", lambda e: e.dma_start(out=stag[:], in_=lA), "d_l")
                b.wait("tensor", "d_l", dl)
                for k in range(32):
                    pe_t0 = b.run("tensor", lambda e, k=k: e.transpose(
                        psT[:], stag[:, 128 * k:128 * (k + 1)], idn), inc="pe")
                    b.wait("vector", "pe", pe_t0)
                    ve_t0 = b.run("vector", lambda e, k=k: e.tensor_copy(
                        afm[:, 128 * k:128 * (k + 1)], psT[:]), inc="ve")
                    b.run("vector", lambda e: e.drain())
                    b.wait("tensor", "ve", ve_t0)
                b.wait("gpsimd", "ve", b.val("ve"))
                dl = b.dma("gpsimd", lambda e: e.dma_start(
                    out=stag.rearrange("p (c r) -> p c r", r=128),
                    in_=lB.rearrange("(c p) r -> p c r", p=128)), "d_l")
                b.wait("tensor", "d_l", b.val("d_l"))
                b.wait("vector", "d_l", b.val("d_l"))
                for k in range(32):
                    pe_t0 = b.run("tensor", lambda e, k=k: e.transpose(
                        psT[:], stag[:, 128 * k:128 * (k + 1)], idn), inc="pe")
                    b.wait("vector", "pe", pe_t0)
                    ve_t0 = b.run("vector", lambda e, k=k: e.tensor_scalar(
                        bblk[:, 128 * k:128 * (k + 1)], psT[:], coefR[:], 2.0,
                        op0=ALU.mult, op1=ALU.mult), inc="ve")
                    b.run("vector", lambda e: e.drain())
                    b.wait("tensor", "ve", ve_t0)

            load_group(0)
            load_group(1)
            work = []
            for g in range(NG):
                for h4 in range(4):
                    for qt in range(NQT):
                        for cg in range(qt + 1):
                            work.append((g, h4, qt, cg))
            qslot_of = {}

            def emit_S(idx):
                g, h4, qt, cg = work[idx]
                pr = idx % 2
                if cg == 0:
                    qs = qts.next()
                    qts.load(qs, lambda e, dd, g=g, h4=h4, qt=qt: e.dma_start(
                        out=dd, in_=Qfm[128 * (4 * g + h4):128 * (4 * g + h4 + 1),
                                        QT * qt:QT * (qt + 1)]))
                    qslot_of[(g, h4, qt)] = qs
                qs = qslot_of[(g, h4, qt)]
                kslot = g % 2
                qts.wait_ready("tensor", qs)
                if cg == 0 and h4 == 0 and qt == 0:
                    kgs.wait_ready("tensor", kslot)
                    vhs.wait_ready("tensor", kslot)
                b.waits("tensor", pairfree[pr])
                pairfree[pr] = []
                for u in range(4):
                    c = 4 * cg + u
                    fn = lambda e, pr=pr, u=u, c=c, kslot=kslot, qs=qs: e.matmul(
                        s_sub(pr, u),
                        kg[:, kslot * TT_ + 128 * c:kslot * TT_ + 128 * (c + 1)],
                        qtl[:, qs * QT:(qs + 1) * QT],
                        start=(u % 2 == 0), stop=(u % 2 == 1))
                    if u < 3:
                        b.run("tensor", fn)
                    else:
                        pe_t = b.run("tensor", fn, inc="pe")
                if cg == qt:
                    qts.consumed(qs, "pe", pe_t)
                pslot = psl.next()
                b.wait("scalar", "pe", pe_t)
                b.waits("scalar", psl.free[pslot])
                psl.free[pslot] = []
                ac_t = b.run("scalar", lambda e, pr=pr, pslot=pslot: e.activation(
                    pbf[:, pslot * 1024:(pslot + 1) * 1024], s_pair(pr), AF.Exp,
                    bias=negm_col[:], scale=SCL), inc="ac")
                pairfree[pr] = [("ac", ac_t)]
                conds = [("ac", ac_t)]
                if cg == qt:
                    b.wait("vector", "ac", ac_t)
                    ve_t = b.run("vector", lambda e, pslot=pslot, qt=qt: e.tensor_tensor(
                        pbf[:, pslot * 1024:(pslot + 1) * 1024],
                        pbf[:, pslot * 1024:(pslot + 1) * 1024],
                        maskb[:, 1024 * qt:1024 * (qt + 1)], op=ALU.mult), inc="ve")
                    b.run("vector", lambda e: e.drain())
                    conds = [("ve", ve_t)]
                psl.produce(pslot, conds)

            def emit_PV(idx):
                g, h4, qt, cg = work[idx]
                pr = idx % 3
                obp = ((4 * g + h4) * NQT + qt) % 2
                kslot = g % 2
                psl.wait_ready("tensor", pr)
                if cg == 0:
                    b.waits("tensor", obankfree[obp])
                    obankfree[obp] = []
                for u in range(4):
                    c = 4 * cg + u
                    for sub in range(2):
                        st_fl = (cg == 0 and u == 0 and sub == 0)
                        sp_fl = (cg == qt and u == 3 and sub == 1)
                        fn = lambda e, pr=pr, u=u, sub=sub, c=c, kslot=kslot, \
                             obp=obp, st_fl=st_fl, sp_fl=sp_fl: e.matmul(
                            ps[:, 512 * (4 + obp) + 129 * sub:
                               512 * (4 + obp) + 129 * (sub + 1)],
                            pbf[:, pr * 1024 + 256 * u + 128 * sub:
                                pr * 1024 + 256 * u + 128 * (sub + 1)],
                            vh[:, kslot * 32 * 129 + 129 * c:
                               kslot * 32 * 129 + 129 * (c + 1)],
                            start=st_fl, stop=sp_fl)
                        if u == 3 and sub == 1:
                            pe_t = b.run("tensor", fn, inc="pe")
                        else:
                            b.run("tensor", fn)
                psl.consumed(pr, "pe", pe_t)
                if cg == qt:
                    os_ = osl.next()
                    b.wait("vector", "pe", pe_t)
                    b.waits("vector", osl.free[os_])
                    osl.free[os_] = []
                    obb = 4 + obp
                    b.run("vector", lambda e, obb=obb: e.reciprocal(
                        recl[:, 0:1], ps[:, 512 * obb + 128:512 * obb + 129]))
                    b.run("vector", lambda e, obb=obb: e.reciprocal(
                        recl[:, 1:2], ps[:, 512 * obb + 257:512 * obb + 258]))
                    b.run("vector", lambda e: e.drain())
                    ovv = oev[:, os_ * QT:(os_ + 1) * QT]
                    b.run("vector", lambda e, obb=obb, ovv=ovv: e.tensor_scalar(
                        ovv[:, 0:128], ps[:, 512 * obb:512 * obb + 128],
                        recl[:, 0:1], None, op0=ALU.mult))
                    ve_t = b.run("vector", lambda e, obb=obb, ovv=ovv: e.tensor_scalar(
                        ovv[:, 128:256], ps[:, 512 * obb + 129:512 * obb + 257],
                        recl[:, 1:2], None, op0=ALU.mult), inc="ve")
                    b.run("vector", lambda e: e.drain())
                    obankfree[obp] = [("ve", ve_t)]
                    b.wait("sync", "ve", ve_t)
                    h = 4 * g + h4
                    osl.store(os_, lambda e, qt=qt, h=h, ovv=ovv: e.dma_start(
                        out=Otm[QT * qt:QT * (qt + 1), 128 * h:128 * (h + 1)]
                        .rearrange("(s p) d -> p s d", p=128),
                        in_=ovv.rearrange("p (s d) -> p s d", s=2)))
                if (h4, qt, cg) == (3, NQT - 1, NQT - 1) and g + 2 < NG:
                    kgs.consumed(kslot, "pe", pe_t)
                    vhs.consumed(kslot, "pe", pe_t)
                    load_group(g + 2)

            for idx in range(len(work)):
                emit_S(idx)
                if idx > 1:
                    emit_PV(idx - 2)
            emit_PV(len(work) - 2)
            emit_PV(len(work) - 1)
            b.wait_group("sync", "s_bo")
        b.barrier("B")

    if "C1" in PHASES:
        with contextlib.ExitStack() as phs:
            wob = phs.enter_context(nc.sbuf_tensor("c1_wo", [128, 2 * 8192], BF16))
            bo_bf = phs.enter_context(nc.sbuf_tensor("c1_bo", [1, DM], BF16))
            b.fence_loads("gpsimd")
            b.dma("gpsimd", lambda e: e.dma_start(out=bo_bf[:], in_=bo), "d_c")
            xrb = phs.enter_context(nc.sbuf_tensor("c1_xr", [128, 4 * 256], F32))
            yeb = phs.enter_context(nc.sbuf_tensor("c1_ye", [128, 4 * 256], F32))
            jnk = phs.enter_context(nc.sbuf_tensor("c1_jk", [128, 256], BF16))
            for c in range(32):
                b.dma("sync", lambda e, c=c: e.dma_start_transpose(
                    bview(2048 * c, 2048), Otm[:, 128 * c:128 * (c + 1)]), "d_t")
            b.wait("tensor", "d_t", b.val("d_t"))
            b.wait("tensor", "d_c", b.val("d_c"))
            wsl = Slots(b, "c1w", 2, lambda s: wob[:, s * 8192:(s + 1) * 8192])
            xsl = Slots(b, "c1x", 4, lambda s: xrb[:, s * 256:(s + 1) * 256])
            ysl = Slots(b, "c1y", 4, lambda s: yeb[:, s * 256:(s + 1) * 256])
            bankfree = {0: [], 1: [], 2: []}
            b.fence_loads("gpsimd")
            pe_t = 0

            def load_x(ow, tt):
                xs = xsl.next()
                xsl.load(xs, lambda e, dd, ow=ow, tt=tt: e.dma_start(
                    out=dd, in_=xq[128 * tt:128 * (tt + 1),
                                   256 * ow:256 * (ow + 1)]))
                return xs

            xq_queue = []
            for ow in range(16):
                s = ow % 2
                wsl.load(s, lambda e, dd, ow=ow: e.dma_start(
                    out=dd.rearrange("p (k o) -> p k o", o=256),
                    in_=wo.rearrange("(k p) o -> p k o", p=128)
                    [:, :, 256 * ow:256 * (ow + 1)]), queue="gpsimd")
                wsl.wait_ready("tensor", s)
                for tt in range(16):
                    bk = (16 * ow + tt) % 3
                    while len(xq_queue) < 3 and (16 * ow + tt) + len(xq_queue) < 256:
                        nn = (16 * ow + tt) + len(xq_queue)
                        xq_queue.append(load_x(nn // 16, nn % 16))
                    xs = xq_queue.pop(0)
                    b.waits("tensor", bankfree[bk])
                    bankfree[bk] = []
                    for k in range(32):
                        b.run("tensor", lambda e, s=s, k=k, tt=tt, bk=bk: e.matmul(
                            bank(bk, 256), bview(2048 * k + 128 * tt, 128),
                            wob[:, s * 8192 + 256 * k:s * 8192 + 256 * (k + 1)],
                            start=(k == 0), stop=False))
                    pe_t = b.run("tensor", lambda e, bk=bk, ow=ow: e.matmul(
                        bank(bk, 256), ones1[:], bo_bf[0:1, 256 * ow:256 * (ow + 1)],
                        start=False, stop=True), inc="pe")
                    ys = ysl.next()
                    b.wait("vector", "pe", pe_t)
                    xsl.wait_ready("vector", xs)
                    b.waits("vector", ysl.free[ys])
                    ysl.free[ys] = []
                    yv = yeb[:, ys * 256:(ys + 1) * 256]
                    xv = xrb[:, xs * 256:(xs + 1) * 256]
                    ve_t = b.run("vector", lambda e, bk=bk, yv=yv, xv=xv:
                                 e.tensor_tensor(yv, bank(bk, 256), xv, op=ALU.add),
                                 inc="ve")
                    bankfree[bk] = [("ve", ve_t)]
                    xsl.consumed(xs, "ve", ve_t)
                    b.run("vector", lambda e: e.drain())
                    col = 16 * tt + ow
                    ve_t = b.run("vector", lambda e, yv=yv, col=col: e.reduce_sum(
                        s1all[:, col:col + 1], yv, axis=AX), inc="ve")
                    b.wait("scalar", "ve", ve_t)
                    ac_t = b.run("scalar", lambda e, yv=yv, col=col: e.activation(
                        jnk[:], yv, AF.Square, accum_out=s2all[:, col:col + 1]),
                        inc="ac")
                    b.wait("sync", "ve", ve_t)
                    ysl.store(ys, lambda e, tt=tt, ow=ow, yv=yv: e.dma_start(
                        out=Ysc[128 * tt:128 * (tt + 1), 256 * ow:256 * (ow + 1)],
                        in_=yv))
                    ysl.consumed(ys, "ac", ac_t)
                wsl.consumed(s, "pe", pe_t)
            b.wait("vector", "ac", b.val("ac"))
            b.run("vector", lambda e: e.drain())
            for tt in range(16):
                b.run("vector", lambda e, tt=tt: e.reduce_sum(
                    ln2mu[:, tt:tt + 1], s1all[:, 16 * tt:16 * (tt + 1)], axis=AX))
            for tt in range(16):
                b.run("vector", lambda e, tt=tt: e.reduce_sum(
                    ln2rs[:, tt:tt + 1], s2all[:, 16 * tt:16 * (tt + 1)], axis=AX))
            b.run("vector", lambda e: e.drain())
            b.run("vector", lambda e: e.tensor_scalar(
                ln2mu[:], ln2mu[:], 1.0 / DM, None, op0=ALU.mult))
            b.run("vector", lambda e: e.drain())
            b.run("vector", lambda e: e.tensor_tensor(
                statc[:], ln2mu[:], ln2mu[:], op=ALU.mult))
            b.run("vector", lambda e: e.drain())
            b.run("vector", lambda e: e.tensor_scalar(
                ln2rs[:], ln2rs[:], 1.0 / DM, None, op0=ALU.mult))
            b.run("vector", lambda e: e.drain())
            ve_t = b.run("vector", lambda e: e.tensor_tensor(
                ln2rs[:], ln2rs[:], statc[:], op=ALU.subtract), inc="ve")
            b.wait("scalar", "ve", ve_t)
            ac_t = b.run("scalar", lambda e: e.activation(
                ln2rs[:], ln2rs[:], AF.Sqrt, bias=eps_col[:]), inc="ac")
            b.wait("vector", "ac", ac_t)
            b.run("vector", lambda e: e.reciprocal(ln2rs[:], ln2rs[:]))
            b.run("vector", lambda e: e.drain())
            b.wait_group("sync", "s_c1y")
        b.barrier("C1")

    if "C2" in PHASES:
        with contextlib.ExitStack() as phs:
          with contextlib.ExitStack() as ph1:
            ytb = ph1.enter_context(nc.sbuf_tensor("c2_yt", [128, 2 * DM], BF16))
            ynb = ph1.enter_context(nc.sbuf_tensor("c2_yn", [128, 2 * DM], BF16))
            ysl = Slots(b, "c2y", 2, lambda s: ytb[:, s * DM:(s + 1) * DM])
            nsl = Slots(b, "c2n", 2, lambda s: ynb[:, s * DM:(s + 1) * DM])
            b.fence_loads("gpsimd")
            for tt in range(16):
                s = ysl.next()
                ysl.load(s, lambda e, dd, tt=tt: e.dma_start(
                    out=dd, in_=Ysc[128 * tt:128 * (tt + 1), :]), queue="gpsimd")
                ns = nsl.next()
                ysl.wait_ready("vector", s)
                b.waits("vector", nsl.free[ns])
                nsl.free[ns] = []
                yv = ytb[:, s * DM:(s + 1) * DM]
                nv = ynb[:, ns * DM:(ns + 1) * DM]
                ve_t = b.run("vector", lambda e, yv=yv, nv=nv, tt=tt: e.tensor_scalar(
                    nv, yv, ln2mu[:, tt:tt + 1], ln2rs[:, tt:tt + 1],
                    op0=ALU.subtract, op1=ALU.mult), inc="ve")
                b.run("vector", lambda e: e.drain())
                ysl.consumed(s, "ve", ve_t)
                b.wait("sync", "ve", ve_t)
                nsl.store(ns, lambda e, tt=tt, nv=nv: e.dma_start(
                    out=YNtm[128 * tt:128 * (tt + 1), :], in_=nv))

          if True:
            b.wait_group("sync", "s_c2n")
            for c in range(32):
                b.dma("sync", lambda e, c=c: e.dma_start_transpose(
                    bview(2048 * c, 2048), YNtm[:, 128 * c:128 * (c + 1)]), "d_t")
            b.wait("tensor", "d_t", b.val("d_t"))
            b.wait("tensor", "d_c", b.val("d_c"))

            for k in range(32):
                for tw in range(4):
                    fn = lambda e, k=k, tw=tw: e.matmul(
                        bank(tw), afm[:, 128 * k:128 * (k + 1)],
                        bview(2048 * k + 512 * tw, 512),
                        start=(k == 0), stop=(k == 31))
                    if k == 31 and tw == 3:
                        pe_t = b.run("tensor", fn, inc="pe")
                    else:
                        b.run("tensor", fn)
            b.wait("vector", "pe", pe_t)
            for tw in range(4):
                b.run("vector", lambda e, tw=tw: e.tensor_copy(
                    h_fm[:, 512 * tw:512 * (tw + 1)], bank(tw)))
            ve_t = b.run("vector", lambda e: e.drain(), inc="ve")
            b.wait("tensor", "ve", ve_t)

            wtb = phs.enter_context(nc.sbuf_tensor("c2_wt", [128, 2 * 8192], BF16,
                                                   is_transpose_dest=True))
            fb_bf = phs.enter_context(nc.sbuf_tensor("c2_fb", [1, DM], BF16))
            b.dma("gpsimd", lambda e: e.dma_start(out=fb_bf[:], in_=fb), "d_c")
            b.wait("tensor", "d_c", b.val("d_c"))
            yrb = phs.enter_context(nc.sbuf_tensor("c2_yr", [128, 4 * 256], F32))
            ob_ = phs.enter_context(nc.sbuf_tensor("c2_ob", [128, 4 * 256], F32))
            wsl = Slots(b, "c2w", 2, lambda s: wtb[:, s * 8192:(s + 1) * 8192])
            yrs = Slots(b, "c2r", 4, lambda s: yrb[:, s * 256:(s + 1) * 256])
            obs = Slots(b, "c2o", 4, lambda s: ob_[:, s * 256:(s + 1) * 256])
            b.wait("sync", "d_wt", b.val("d_wt"))
            b.fence_loads("sync")
            bankfree = {0: [], 1: [], 2: []}

            def load_wt(ow):
                s = ow % 2
                b.waits("sync", wsl.free[s])
                wsl.free[s] = []
                tgt = None
                for k in range(32):
                    tgt = b.dma("sync", lambda e, ow=ow, k=k, s=s: e.dma_start_transpose(
                        wtb[:, s * 8192 + 256 * k:s * 8192 + 256 * (k + 1)],
                        Wt[256 * ow:256 * (ow + 1), 128 * k:128 * (k + 1)]),
                        f"d_wl{s}")
                wsl.ready[s] = [(f"d_wl{s}", tgt)]

            load_wt(0)
            pe_t = 0

            def load_y2(ow, tt):
                ys = yrs.next()
                yrs.load(ys, lambda e, dd, ow=ow, tt=tt: e.dma_start(
                    out=dd, in_=Ysc[128 * tt:128 * (tt + 1),
                                    256 * ow:256 * (ow + 1)]))
                return ys

            y_queue = []
            for ow in range(16):
                s = ow % 2
                if ow + 1 < 16:
                    load_wt(ow + 1)
                wsl.wait_ready("tensor", s)
                for tt in range(16):
                    bk = 4 + (16 * ow + tt) % 3
                    while len(y_queue) < 3 and (16 * ow + tt) + len(y_queue) < 256:
                        nn = (16 * ow + tt) + len(y_queue)
                        y_queue.append(load_y2(nn // 16, nn % 16))
                    ys = y_queue.pop(0)
                    b.waits("tensor", bankfree[bk - 4])
                    bankfree[bk - 4] = []
                    for k in range(32):
                        b.run("tensor", lambda e, s=s, k=k, tt=tt, bk=bk: e.matmul(
                            bank(bk, 256), bview(2048 * k + 128 * tt, 128),
                            wtb[:, s * 8192 + 256 * k:s * 8192 + 256 * (k + 1)],
                            start=(k == 0), stop=False))
                    b.run("tensor", lambda e, bk=bk, tt=tt, ow=ow: e.matmul(
                        bank(bk, 256), h_fm[:, 128 * tt:128 * (tt + 1)],
                        bblk[:, 256 * ow:256 * (ow + 1)], start=False, stop=False))
                    pe_t = b.run("tensor", lambda e, bk=bk, ow=ow: e.matmul(
                        bank(bk, 256), ones1[:], fb_bf[0:1, 256 * ow:256 * (ow + 1)],
                        start=False, stop=True), inc="pe")
                    os_ = obs.next()
                    b.wait("vector", "pe", pe_t)
                    yrs.wait_ready("vector", ys)
                    b.waits("vector", obs.free[os_])
                    obs.free[os_] = []
                    ovv = ob_[:, os_ * 256:(os_ + 1) * 256]
                    yvv = yrb[:, ys * 256:(ys + 1) * 256]
                    ve_t = b.run("vector", lambda e, bk=bk, ovv=ovv, yvv=yvv:
                                 e.tensor_tensor(ovv, bank(bk, 256), yvv, op=ALU.add),
                                 inc="ve")
                    b.run("vector", lambda e: e.drain())
                    bankfree[bk - 4] = [("ve", ve_t)]
                    yrs.consumed(ys, "ve", ve_t)
                    b.wait("sync", "ve", ve_t)
                    obs.store(os_, lambda e, tt=tt, ow=ow, ovv=ovv: e.dma_start(
                        out=out[128 * tt:128 * (tt + 1), 256 * ow:256 * (ow + 1)],
                        in_=ovv))
                wsl.consumed(s, "pe", pe_t)
            b.wait_group("sync", "s_c2o")

    taps = []
    if "xn" in DEBUG_TAPS:
        taps += [("t_xnkv", xn_kv), ("t_xnq", xn_q)]
    if "kv" in DEBUG_TAPS:
        taps += [("t_kfm", Kfm), ("t_vtm", Vtm)]
    if "q" in DEBUG_TAPS:
        taps += [("t_qfm", Qfm)]
    if "o" in DEBUG_TAPS:
        taps += [("t_otm", Otm)]
    if "y" in DEBUG_TAPS:
        taps += [("t_ysc", Ysc)]
    for name, src in taps:
        dstp = nc.dram_tensor(name, list(src.shape), src.dtype,
                              kind="ExternalOutput").ap()
        b.dma("sync", lambda e, dstp=dstp, src=src: e.dma_start(out=dstp, in_=src),
              "d_tap")
    if taps:
        b.wait("sync", "d_tap", b.val("d_tap"))

    with nc.Block() as block:
        @block.sync
        def _(e):
            for f in b.ops["sync"]:
                f(e)
        @block.tensor
        def _(e):
            for f in b.ops["tensor"]:
                f(e)
        @block.vector
        def _(e):
            for f in b.ops["vector"]:
                f(e)
        @block.scalar
        def _(e):
            for f in b.ops["scalar"]:
                f(e)
        @block.gpsimd
        def _(e):
            for f in b.ops["gpsimd"]:
                f(e)
    return nc


def prepare_inputs(inputs):
    x = np.asarray(inputs["L0_attn_x"], np.float32)
    Wq = np.asarray(inputs["L0_attn_Wq"], np.float32)
    Wk = np.asarray(inputs["L0_attn_Wk"], np.float32)
    Wv = np.asarray(inputs["L0_attn_Wv"], np.float32)
    Wo = np.asarray(inputs["L0_attn_Wo"], np.float32)
    bo = np.asarray(inputs["L0_attn_bo"], np.float32).reshape(1, DM)
    lora_A = np.asarray(inputs["L0_ffn_lora_A"], np.float32)
    lora_B = np.asarray(inputs["L0_ffn_lora_B"], np.float32)
    W = np.asarray(inputs["L0_ffn_W"], np.float32)
    fbias = np.asarray(inputs["L0_ffn_bias"], np.float32).reshape(1, DM)
    lwt = np.asarray(inputs["L0_ffn_lora_weights"], np.float32)

    pv = _perm_vec()
    qcols = np.concatenate([128 * h + pv for h in range(NH)])
    kcols = np.concatenate([128 * g + pv for g in range(NG)])
    wq_p = np.ascontiguousarray(Wq[:, qcols])
    wk_p = np.ascontiguousarray(Wk[:, kcols])
    wv_c = np.ascontiguousarray(Wv)
    lA_ = np.ascontiguousarray(lora_A.reshape(128, DM))
    lB_ = np.ascontiguousarray(lora_B.transpose(1, 0, 2).reshape(DM, 128))
    lw_ = np.ascontiguousarray(np.repeat(lwt, 16).reshape(128, 1))
    tka, tkb = _trig(np.arange(TT_))
    iden = np.eye(128, dtype=np.float32)
    swpm = np.roll(iden, 64, axis=0).astype(np.float32)

    in_maps = []
    for c in range(8):
        bb, j = c // 2, c % 2
        rows = np.concatenate([np.arange(QT * (2 * i + j), QT * (2 * i + j + 1))
                               for i in range(NQT)])
        tqa_, tqb_ = _trig(rows)
        in_maps.append(dict(
            xkv=np.ascontiguousarray(x[bb]),
            xq=np.ascontiguousarray(x[bb][rows]),
            wq=wq_p, wk=wk_p, wv=wv_c, wo=np.ascontiguousarray(Wo), bo=bo,
            wf=np.ascontiguousarray(W), fb=fbias, lA=lA_, lB=lB_, lw=lw_,
            tqa=np.ascontiguousarray(tqa_), tqb=np.ascontiguousarray(tqb_),
            tka=np.ascontiguousarray(tka), tkb=np.ascontiguousarray(tkb),
            msk=_masks(j), iden=iden, swp=swpm,
        ))
    return in_maps


_NC_CACHE = {}


def kernel(**inputs):
    in_maps = prepare_inputs(inputs)
    key = (tuple(PHASES), tuple(DEBUG_TAPS))
    if key not in _NC_CACHE:
        _NC_CACHE[key] = build_nc()
    nc = _NC_CACHE[key]
    res = run_bass_kernel_spmd(nc, in_maps, core_ids=list(range(8)))
    full = np.empty((BT, TT_, DM), np.float32)
    for c in range(8):
        bb, j = c // 2, c % 2
        o = res.results[c]["out"]
        for i in range(NQT):
            full[bb, QT * (2 * i + j):QT * (2 * i + j + 1)] = o[QT * i:QT * (i + 1)]
    return full


# version 22
# speedup vs baseline: 1.2099x; 1.2099x over previous
import contextlib
import numpy as np
import concourse.bass as bass
import concourse.mybir as mybir
from concourse.bass_utils import run_bass_kernel_spmd

F32 = mybir.dt.float32
BF16 = mybir.dt.bfloat16
AF = mybir.ActivationFunctionType
ALU = mybir.AluOpType
AX = mybir.AxisListType.X

BT, TT_, DM = 4, 4096, 4096
NH, NG, HD = 32, 8, 128
TQ = 2048
QT = 256
NQT = TQ // QT
MSHIFT = 30.0
SCL = float(1.0 / np.sqrt(HD))
EPS = 1e-5
ROPE_THETA = 10000.0

ENGS = ("sync", "tensor", "vector", "scalar", "gpsimd")

PHASES = ["A1", "A2", "A3", "B", "C1", "C2"]
DEBUG_TAPS = []


class Builder:
    def __init__(self, nc, es):
        self.nc = nc
        self.es = es
        self.ops = {e: [] for e in ENGS}
        self.sems = {}
        self.cnt = {}
        self.lastw = {}

    def sem(self, name):
        if name not in self.sems:
            self.sems[name] = self.es.enter_context(self.nc.semaphore(name))
            self.cnt[name] = 0
        return self.sems[name]

    def bump(self, name, n=1):
        self.sem(name)
        self.cnt[name] += n
        return self.cnt[name]

    def val(self, name):
        self.sem(name)
        return self.cnt[name]

    def on(self, eng, fn):
        self.ops[eng].append(fn)

    def wait(self, eng, sname, target):
        if target <= 0:
            return
        if self.lastw.get((eng, sname), -1) >= target:
            return
        self.lastw[(eng, sname)] = target
        h = self.sem(sname)
        self.on(eng, lambda e, h=h, t=target: e.wait_ge(h, t))

    def waits(self, eng, conds):
        for sname, tgt in conds:
            self.wait(eng, sname, tgt)

    def run(self, eng, fn, inc=None, incn=1):
        if inc is None:
            self.on(eng, fn)
            return None
        h = self.sem(inc)
        tgt = self.bump(inc, incn)
        self.on(eng, lambda e, fn=fn, h=h, n=incn: fn(e).then_inc(h, n))
        return tgt

    def dma(self, queue, fn, dsem):
        h = self.sem(dsem)
        tgt = self.bump(dsem, 16)
        self.on(queue, lambda e, fn=fn, h=h: fn(e).then_inc(h, 16))
        return tgt

    def fence_loads(self, queue):
        for sname in ("pe", "ve", "ac"):
            if sname in self.cnt:
                self.wait(queue, sname, self.cnt[sname])
        self.wait_group(queue, "s_")

    def wait_group(self, eng, prefix):
        for sname in list(self.cnt):
            if sname.startswith(prefix):
                self.wait(eng, sname, self.cnt[sname])

    def barrier(self, tag=""):
        h = self.sem("ph")
        self.cnt["ph"] += 5
        tgt = self.cnt["ph"]
        for eng in ENGS:
            if eng == "sync":
                self.on(eng, lambda e, h=h: e.sem_inc(h, 1))
            else:
                self.on(eng, lambda e, h=h: e.drain().then_inc(h, 1))
            self.on(eng, lambda e, h=h, t=tgt: e.wait_ge(h, t))


class Slots:
    def __init__(self, b, name, n, view):
        self.b, self.name, self.n, self.view = b, name, n, view
        self.ready = [[] for _ in range(n)]
        self.free = [[] for _ in range(n)]
        self.i = 0

    def next(self):
        s = self.i % self.n
        self.i += 1
        return s

    def load(self, s, fn, queue="sync"):
        b = self.b
        b.waits(queue, self.free[s])
        self.free[s] = []
        dsem = f"d_{self.name}{s}"
        tgt = b.dma(queue, lambda e, fn=fn, d=self.view(s): fn(e, d), dsem)
        self.ready[s] = [(dsem, tgt)]
        return tgt

    def produce(self, s, conds):
        self.free[s] = []
        self.ready[s] = list(conds)

    def wait_ready(self, eng, s):
        self.b.waits(eng, self.ready[s])

    def consumed(self, s, sname, tgt):
        self.free[s].append((sname, tgt))

    def store(self, s, fn, queue="sync"):
        b = self.b
        dsem = f"s_{self.name}{s}"
        tgt = b.dma(queue, fn, dsem)
        self.consumed(s, dsem, tgt)
        return dsem, tgt


def _perm_vec():
    return np.concatenate([np.arange(0, HD, 2), np.arange(1, HD, 2)])


def _trig(positions):
    j = np.arange(HD // 2)
    freqs = 1.0 / ROPE_THETA ** (2.0 * j / HD)
    ang = positions.astype(np.float64)[None, :] * freqs[:, None].astype(np.float64)
    cos = np.cos(ang).astype(np.float32)
    sin = np.sin(ang).astype(np.float32)
    return (np.concatenate([cos, cos], 0), np.concatenate([-sin, sin], 0))


def _masks(j):
    out = np.zeros((NQT, 4, 128, QT), np.float32)
    t_id = np.arange(QT)[None, :]
    s_id = np.arange(128)[:, None]
    for i in range(NQT):
        g = 2 * i + j
        for r in range(4):
            c = 4 * i + r
            out[i, r] = (128 * c + s_id <= QT * g + t_id)
    return out.reshape(NQT * 4 * 128, QT)


def build_nc():
    nc = bass.Bass("TRN2", target_bir_lowering=False)
    es = contextlib.ExitStack()
    nc._es = es

    d = {}
    def din(name, shape, dtype=F32):
        d[name] = nc.dram_tensor(name, list(shape), dtype, kind="ExternalInput").ap()
        return d[name]
    def dscr(name, shape, dtype):
        d[name] = nc.dram_tensor(name, list(shape), dtype, kind="Internal").ap()
        return d[name]

    xkv = din("xkv", (TT_, DM))
    xq = din("xq", (TQ, DM))
    wq = din("wq", (DM, NH * HD))
    wk = din("wk", (DM, NG * HD))
    wv = din("wv", (DM, NG * HD))
    wo = din("wo", (NH * HD, DM))
    bo = din("bo", (1, DM))
    wf = din("wf", (DM, DM))
    fb = din("fb", (1, DM))
    lA = din("lA", (128, DM))
    lB = din("lB", (DM, 128))
    lw = din("lw", (128, 1))
    tqa = din("tqa", (128, TQ)); tqb = din("tqb", (128, TQ))
    tka = din("tka", (128, TT_)); tkb = din("tkb", (128, TT_))
    msk = din("msk", (NQT * 4 * 128, QT))
    iden = din("iden", (128, 128))
    swp = din("swp", (128, 128))
    out = nc.dram_tensor("out", [TQ, DM], F32, kind="ExternalOutput").ap()

    xn_kv = dscr("xn_kv", (TT_, DM), BF16)
    xn_q = dscr("xn_q", (TQ, DM), BF16)
    Kfm = dscr("Kfm", (NG * HD, TT_), BF16)
    Vtm = dscr("Vtm", (TT_, NG * HD), BF16)
    Qfm = dscr("Qfm", (NH * HD, TQ), BF16)
    Otm = dscr("Otm", (TQ, NH * HD), BF16)
    Ysc = dscr("Ysc", (TQ, DM), F32)
    YNtm = dscr("YNtm", (TQ, DM), BF16)
    Wt = dscr("Wt", (DM, DM), BF16)

    b = Builder(nc, es)
    sbt = lambda name, shape, dtype=BF16, **kw: es.enter_context(
        nc.sbuf_tensor(name, list(shape), dtype, **kw))

    bigA = sbt("bigA", (128, 65536), BF16, is_transpose_dest=True)
    ones1 = sbt("ones1", (1, 128), BF16)
    eps_col = sbt("eps_col", (128, 1), F32)
    negm_col = sbt("negm_col", (128, 1), F32)
    statc = sbt("statc", (128, 16), F32)
    ln2mu = sbt("ln2mu", (128, 16), F32)
    ln2rs = sbt("ln2rs", (128, 16), F32)
    s1all = sbt("s1all", (128, 256), F32)
    s2all = sbt("s2all", (128, 256), F32)
    h_fm = sbt("h_fm", (128, TQ), BF16)
    swapm = sbt("swapm", (128, 128), BF16)
    afm = sbt("afm", (128, 32 * 128), BF16)
    bblk = sbt("bblk", (128, DM), BF16)
    coefR = sbt("coefR", (128, 1), F32)
    ps = es.enter_context(nc.psum_tensor("ps", [128, 3584], F32))
    psT = es.enter_context(nc.psum_tensor("psT", [128, 128], BF16))
    bank = lambda i, n=512: ps[:, 512 * i:512 * i + n]
    bview = lambda off, ln: bigA[:, off:off + ln]

    b.run("vector", lambda e: e.memset(ones1[:], 1.0))
    b.run("vector", lambda e: e.memset(eps_col[:], EPS))
    b.run("vector", lambda e: e.memset(negm_col[:], -MSHIFT), inc="ve")
    b.dma("gpsimd", lambda e: e.dma_start(out=swapm[:], in_=swp), "d_c")
    for i in range(8):
        b.dma("gpsimd", lambda e, i=i: e.dma_start(
            out=Wt[512 * i:512 * (i + 1), :], in_=wf[512 * i:512 * (i + 1), :]), "d_wt")
    b.barrier("init")

    if "A1" in PHASES:
        with contextlib.ExitStack() as phs:
            xt = phs.enter_context(nc.sbuf_tensor("a1_xt", [128, 3 * DM], BF16))
            junk = phs.enter_context(nc.sbuf_tensor("a1_junk", [128, DM], BF16))
            xno = phs.enter_context(nc.sbuf_tensor("a1_xno", [128, 2 * DM], BF16))
            xsl = Slots(b, "a1x", 3, lambda s: xt[:, s * DM:(s + 1) * DM])
            osl = Slots(b, "a1o", 2, lambda s: xno[:, s * DM:(s + 1) * DM])
            tiles = [(xkv, xn_kv, i) for i in range(TT_ // 128)] + \
                    [(xq, xn_q, i) for i in range(TQ // 128)]
            po_hist = []

            def load_a1(n):
                src_, dst_, i_ = tiles[n]
                ss = xsl.next()
                xsl.load(ss, lambda e, dd, src2=src_, i2=i_: e.dma_start(
                    out=dd, in_=src2[128 * i2:128 * (i2 + 1), :]), queue="gpsimd")
                return ss

            def stageA(n, slot):
                sc0 = 8 * (n % 2)
                xv = xt[:, slot * DM:(slot + 1) * DM]
                xsl.wait_ready("scalar", slot)
                ac_t = b.run("scalar", lambda e, xv=xv, sc0=sc0: e.activation(
                    junk[:, 0:DM], xv, AF.Square, accum_out=statc[:, sc0 + 1:sc0 + 2]),
                    inc="ac")
                xsl.wait_ready("vector", slot)
                b.run("vector", lambda e, xv=xv, sc0=sc0: e.reduce_sum(
                    statc[:, sc0:sc0 + 1], xv, axis=AX))
                b.run("vector", lambda e: e.drain())
                return ac_t

            def stageB(n, slot, ac_sq):
                sc0 = 8 * (n % 2)
                src_, dst_, i_ = tiles[n]
                xv = xt[:, slot * DM:(slot + 1) * DM]
                b.wait("vector", "ac", ac_sq)
                b.run("vector", lambda e, sc0=sc0: e.tensor_scalar(
                    statc[:, sc0 + 2:sc0 + 3], statc[:, sc0:sc0 + 1], 1.0 / DM,
                    None, op0=ALU.mult))
                b.run("vector", lambda e: e.drain())
                b.run("vector", lambda e, sc0=sc0: e.tensor_tensor(
                    statc[:, sc0 + 3:sc0 + 4], statc[:, sc0 + 2:sc0 + 3],
                    statc[:, sc0 + 2:sc0 + 3], op=ALU.mult))
                b.run("vector", lambda e: e.drain())
                ve_t = b.run("vector", lambda e, sc0=sc0: e.tensor_scalar(
                    statc[:, sc0 + 4:sc0 + 5], statc[:, sc0 + 1:sc0 + 2], 1.0 / DM,
                    statc[:, sc0 + 3:sc0 + 4], op0=ALU.mult, op1=ALU.subtract),
                    inc="ve")
                b.wait("scalar", "ve", ve_t)
                ac_t = b.run("scalar", lambda e, sc0=sc0: e.activation(
                    statc[:, sc0 + 5:sc0 + 6], statc[:, sc0 + 4:sc0 + 5], AF.Sqrt,
                    bias=eps_col[:]), inc="ac")
                b.wait("vector", "ac", ac_t)
                b.run("vector", lambda e, sc0=sc0: e.reciprocal(
                    statc[:, sc0 + 6:sc0 + 7], statc[:, sc0 + 5:sc0 + 6]))
                b.run("vector", lambda e: e.drain())
                ve_t = b.run("vector", lambda e, sc0=sc0: e.tensor_scalar(
                    statc[:, sc0 + 7:sc0 + 8], statc[:, sc0 + 2:sc0 + 3],
                    statc[:, sc0 + 6:sc0 + 7], -1.0,
                    op0=ALU.mult, op1=ALU.mult), inc="ve")
                b.run("vector", lambda e: e.drain())
                o = osl.next()
                b.wait("scalar", "ve", ve_t)
                b.waits("scalar", osl.free[o])
                osl.free[o] = []
                ov = xno[:, o * DM:(o + 1) * DM]
                po_t = b.run("scalar", lambda e, xv=xv, ov=ov, sc0=sc0: e.activation(
                    ov, xv, AF.Identity, bias=statc[:, sc0 + 7:sc0 + 8],
                    scale=statc[:, sc0 + 6:sc0 + 7]), inc="ac")
                xsl.consumed(slot, "ac", po_t)
                po_hist.append(po_t)
                b.wait("sync", "ac", po_t)
                osl.store(o, lambda e, dst=dst_, i=i_, ov=ov: e.dma_start(
                    out=dst[128 * i:128 * (i + 1), :], in_=ov))

            a1q = []
            prev = None
            for n_ in range(len(tiles)):
                while len(a1q) < 2 and n_ + len(a1q) < len(tiles):
                    a1q.append(load_a1(n_ + len(a1q)))
                slot = a1q.pop(0)
                ac_sq = stageA(n_, slot)
                if prev is not None:
                    stageB(*prev)
                prev = (n_, slot, ac_sq)
            stageB(*prev)
            b.wait_group("sync", "s_a1o")
        b.barrier("A1")

    def proj_fm(tag, phs, w_dram, n_oc, nrows, dst_fm, trig_sb, trig_off, abs0):
        wbuf = phs.enter_context(nc.sbuf_tensor(f"{tag}_wb", [128, 2 * 4096], BF16))
        ev = phs.enter_context(nc.sbuf_tensor(f"{tag}_ev", [128, 4 * 512], BF16))
        rt = phs.enter_context(nc.sbuf_tensor(f"{tag}_rt", [128, 2 * 512], BF16))
        ko = phs.enter_context(nc.sbuf_tensor(f"{tag}_ko", [128, 4 * 512], BF16))
        wsl = Slots(b, f"{tag}w", 2, lambda s: wbuf[:, s * 4096:(s + 1) * 4096])
        esl = Slots(b, f"{tag}e", 4, lambda s: ev[:, s * 512:(s + 1) * 512])
        ksl = Slots(b, f"{tag}k", 4, lambda s: ko[:, s * 512:(s + 1) * 512])
        ntw = nrows // 512
        span = trig_sb.shape[1] // 2
        bankfree = {0: [], 1: [], 2: []}
        swapfree = {3: [], 4: [], 5: []}
        nswap = [0]
        pend = []
        b.fence_loads("gpsimd")
        b.wait("tensor", "d_c", b.val("d_c"))

        def flush_rope(item):
            es_, evv, ac_t, oc, tw = item
            sbk = 3 + nswap[0] % 3
            nswap[0] += 1
            b.wait("tensor", "ac", ac_t)
            b.waits("tensor", swapfree[sbk])
            swapfree[sbk] = []
            pe_t2 = b.run("tensor", lambda e, sbk=sbk, evv=evv: e.matmul(
                bank(sbk), swapm[:], evv, start=True, stop=True), inc="pe")
            ks = ksl.next()
            b.wait("vector", "ac", ac_t)
            b.waits("vector", ksl.free[ks])
            ksl.free[ks] = []
            kov = ko[:, ks * 512:(ks + 1) * 512]
            toff = trig_off + 512 * tw
            b.run("vector", lambda e, evv=evv, toff=toff: e.tensor_tensor(
                rt[:, 0:512], evv, trig_sb[:, toff:toff + 512], op=ALU.mult))
            b.wait("vector", "pe", pe_t2)
            ve_s = b.run("vector", lambda e, sbk=sbk, toff=toff, span=span:
                         e.tensor_tensor(
                rt[:, 512:1024], bank(sbk),
                trig_sb[:, span + toff:span + toff + 512], op=ALU.mult), inc="ve")
            swapfree[sbk] = [("ve", ve_s)]
            b.run("vector", lambda e: e.drain())
            ve_t = b.run("vector", lambda e, kov=kov: e.tensor_tensor(
                kov, rt[:, 0:512], rt[:, 512:1024], op=ALU.add), inc="ve")
            b.run("vector", lambda e: e.drain())
            esl.consumed(es_, "ve", ve_t)
            esl.consumed(es_, "pe", pe_t2)
            b.wait("sync", "ve", ve_t)
            p0 = abs0 + 512 * tw
            ksl.store(ks, lambda e, oc=oc, p0=p0, kov=kov: e.dma_start(
                out=dst_fm[128 * oc:128 * (oc + 1), p0:p0 + 512], in_=kov))
        for oc in range(n_oc):
            s = oc % 2
            wsl.load(s, lambda e, dd, oc=oc: e.dma_start(
                out=dd.rearrange("p (k o) -> p k o", o=128),
                in_=w_dram.rearrange("(k p) o -> p k o", p=128)[:, :, 128 * oc:128 * (oc + 1)]),
                queue="gpsimd")
            wsl.wait_ready("tensor", s)
            pe_t = None
            for tw in range(ntw):
                bk = (oc * ntw + tw) % 3
                b.waits("tensor", bankfree[bk])
                bankfree[bk] = []
                for k in range(32):
                    fn = lambda e, s=s, k=k, tw=tw, bk=bk: e.matmul(
                        bank(bk), wbuf[:, s * 4096 + 128 * k:s * 4096 + 128 * (k + 1)],
                        bview(nrows * k + 512 * tw, 512),
                        start=(k == 0), stop=(k == 31))
                    if k < 31:
                        b.run("tensor", fn)
                    else:
                        pe_t = b.run("tensor", fn, inc="pe")
                es_ = esl.next()
                b.wait("scalar", "pe", pe_t)
                b.waits("scalar", esl.free[es_])
                esl.free[es_] = []
                evv = ev[:, es_ * 512:(es_ + 1) * 512]
                ac_t = b.run("scalar", lambda e, bk=bk, evv=evv: e.activation(
                    evv, bank(bk), AF.Copy), inc="ac")
                bankfree[bk] = [("ac", ac_t)]
                pend.append((es_, evv, ac_t, oc, tw))
                if len(pend) >= 2:
                    flush_rope(pend.pop(0))
            wsl.consumed(s, "pe", pe_t)
        while pend:
            flush_rope(pend.pop(0))

    if "A2" in PHASES:
        with contextlib.ExitStack() as phs:
            trig = phs.enter_context(nc.sbuf_tensor("a2_trig", [128, 2 * 2048], BF16))
            for w in range(2):
                if w > 0:
                    b.wait("sync", "pe", b.val("pe"))
                b.wait("gpsimd", "ve", b.val("ve"))
                b.dma("gpsimd", lambda e, w=w: e.dma_start(
                    out=trig[:, 0:2048], in_=tka[:, 2048 * w:2048 * (w + 1)]), "d_tr")
                b.dma("gpsimd", lambda e, w=w: e.dma_start(
                    out=trig[:, 2048:4096], in_=tkb[:, 2048 * w:2048 * (w + 1)]), "d_tr")
                b.wait("vector", "d_tr", b.val("d_tr"))
                for c in range(32):
                    b.dma("sync", lambda e, c=c, w=w: e.dma_start_transpose(
                        bview(2048 * c, 2048),
                        xn_kv[2048 * w:2048 * (w + 1), 128 * c:128 * (c + 1)]), "d_t")
                b.wait("tensor", "d_t", b.val("d_t"))
                with contextlib.ExitStack() as kps:
                    proj_fm(f"a2w{w}", kps, wk, NG, 2048, Kfm, trig,
                            0, 2048 * w)
                with contextlib.ExitStack() as vps:
                    vwb = vps.enter_context(
                        nc.sbuf_tensor(f"a2v{w}_w", [128, 32 * 512], BF16))
                    vev = vps.enter_context(
                        nc.sbuf_tensor(f"a2v{w}_e", [128, 4 * 512], BF16))
                    vsl = Slots(b, f"a2v{w}e", 4, lambda s: vev[:, s * 512:(s + 1) * 512])
                    vbank = {0: [("ac", b.val("ac"))], 1: [], 2: []}
                    for ow in range(2):
                        b.fence_loads("gpsimd")
                        dw = b.dma("gpsimd", lambda e, ow=ow: e.dma_start(
                            out=vwb.rearrange("p (k o) -> p k o", o=512),
                            in_=wv.rearrange("(k p) o -> p k o", p=128)
                            [:, :, 512 * ow:512 * (ow + 1)]), f"d_vw{w}")
                        b.wait("tensor", f"d_vw{w}", dw)
                        for tt in range(16):
                            bk = tt % 3
                            b.waits("tensor", vbank[bk])
                            vbank[bk] = []
                            for k in range(32):
                                fn = lambda e, k=k, tt=tt, bk=bk: e.matmul(
                                    bank(bk), bview(2048 * k + 128 * tt, 128),
                                    vwb[:, 512 * k:512 * (k + 1)],
                                    start=(k == 0), stop=(k == 31))
                                if k < 31:
                                    b.run("tensor", fn)
                                else:
                                    pe_t = b.run("tensor", fn, inc="pe")
                            vs = vsl.next()
                            b.wait("scalar", "pe", pe_t)
                            b.waits("scalar", vsl.free[vs])
                            vsl.free[vs] = []
                            vv = vev[:, vs * 512:(vs + 1) * 512]
                            ac_t = b.run("scalar", lambda e, bk=bk, vv=vv: e.activation(
                                vv, bank(bk), AF.Copy), inc="ac")
                            vbank[bk] = [("ac", ac_t)]
                            b.wait("sync", "ac", ac_t)
                            vsl.store(vs, lambda e, w=w, tt=tt, ow=ow, vv=vv:
                                      e.dma_start(
                                out=Vtm[2048 * w + 128 * tt:2048 * w + 128 * (tt + 1),
                                        512 * ow:512 * (ow + 1)], in_=vv))
            b.wait_group("sync", "s_a2")
        b.barrier("A2")

    if "A3" in PHASES:
        with contextlib.ExitStack() as phs:
            trigq = phs.enter_context(nc.sbuf_tensor("a3_trig", [128, 2 * TQ], BF16))
            b.dma("gpsimd", lambda e: e.dma_start(out=trigq[:, 0:TQ], in_=tqa), "d_tr")
            b.dma("gpsimd", lambda e: e.dma_start(out=trigq[:, TQ:2 * TQ], in_=tqb), "d_tr")
            for c in range(32):
                b.dma("sync", lambda e, c=c: e.dma_start_transpose(
                    bview(2048 * c, 2048), xn_q[:, 128 * c:128 * (c + 1)]), "d_t")
            b.wait("tensor", "d_t", b.val("d_t"))
            b.wait("vector", "d_tr", b.val("d_tr"))
            proj_fm("a3", phs, wq, NH, 2048, Qfm, trigq, 0, 0)
            b.wait_group("sync", "s_a3")
        b.barrier("A3")

    if "B" in PHASES:
        with contextlib.ExitStack() as phs:
            maskb = bigA[:, 8192:8192 + NQT * 1024]
            kg = phs.enter_context(nc.sbuf_tensor("b_kg", [128, 2 * TT_], BF16))
            vh = phs.enter_context(nc.sbuf_tensor("b_vh", [128, 2 * 32 * 129], BF16))
            qtl = phs.enter_context(nc.sbuf_tensor("b_qt", [128, 4 * QT], BF16))
            pbf = phs.enter_context(nc.sbuf_tensor("b_p", [128, 3 * 1024], BF16))
            oev = phs.enter_context(nc.sbuf_tensor("b_oev", [128, 2 * QT], BF16))
            recl = phs.enter_context(nc.sbuf_tensor("b_recl", [128, 4], F32))
            b.fence_loads("gpsimd")
            b.dma("gpsimd", lambda e: e.dma_start(
                out=maskb.rearrange("p (i r t) -> p i r t", i=NQT, r=4),
                in_=msk.rearrange("(i r p) t -> p i r t", p=128, r=4)), "d_m")
            b.run("vector", lambda e: e.memset(
                vh[:, 0:32 * 129].rearrange("p (c o) -> p c o", o=129)[:, :, 128:129],
                1.0))
            b.run("vector", lambda e: e.memset(
                vh[:, 32 * 129:64 * 129].rearrange("p (c o) -> p c o", o=129)
                [:, :, 128:129], 1.0), inc="ve")
            b.run("vector", lambda e: e.drain())
            b.wait("vector", "d_m", b.val("d_m"))
            vh_ready_ve = b.val("ve")

            kgs = Slots(b, "bkg", 2, lambda s: kg[:, s * TT_:(s + 1) * TT_])
            vhs = Slots(b, "bvh", 2, lambda s: vh[:, s * 32 * 129:(s + 1) * 32 * 129])
            qts = Slots(b, "bqt", 4, lambda s: qtl[:, s * QT:(s + 1) * QT])
            psl = Slots(b, "bp", 3, lambda s: pbf[:, s * 1024:(s + 1) * 1024])
            osl = Slots(b, "bo", 2, lambda s: oev[:, s * QT:(s + 1) * QT])
            s_sub = lambda p, u: ps[:, 1024 * p + 256 * u:1024 * p + 256 * (u + 1)]
            s_pair = lambda p: ps[:, 1024 * p:1024 * p + 1024]
            pairfree = [[], []]
            obankfree = [[], []]

            def load_group(g):
                s = g % 2
                kgs.load(s, lambda e, dd, g=g: e.dma_start(
                    out=dd, in_=Kfm[128 * g:128 * (g + 1), :]))
                t = vhs.load(s, lambda e, dd, g=g: e.dma_start(
                    out=dd.rearrange("p (c o) -> p c o", o=129)[:, :, 0:128],
                    in_=Vtm.rearrange("(c p) d -> p c d", p=128)
                    [:, :, 128 * g:128 * (g + 1)]))
                vhs.ready[s].append(("ve", vh_ready_ve))

            if True:
                idn = bigA[:, 4096:4224]
                stag = bigA[:, 0:4096]
                b.dma("gpsimd", lambda e: e.dma_start(out=idn, in_=iden), "d_l")
                b.dma("gpsimd", lambda e: e.dma_start(out=coefR[:], in_=lw), "d_l")
                dl = b.dma("gpsimd", lambda e: e.dma_start(out=stag[:], in_=lA), "d_l")
                b.wait("tensor", "d_l", dl)
                for k in range(32):
                    pe_t0 = b.run("tensor", lambda e, k=k: e.transpose(
                        psT[:], stag[:, 128 * k:128 * (k + 1)], idn), inc="pe")
                    b.wait("vector", "pe", pe_t0)
                    ve_t0 = b.run("vector", lambda e, k=k: e.tensor_copy(
                        afm[:, 128 * k:128 * (k + 1)], psT[:]), inc="ve")
                    b.run("vector", lambda e: e.drain())
                    b.wait("tensor", "ve", ve_t0)
                b.wait("gpsimd", "ve", b.val("ve"))
                dl = b.dma("gpsimd", lambda e: e.dma_start(
                    out=stag.rearrange("p (c r) -> p c r", r=128),
                    in_=lB.rearrange("(c p) r -> p c r", p=128)), "d_l")
                b.wait("tensor", "d_l", b.val("d_l"))
                b.wait("vector", "d_l", b.val("d_l"))
                for k in range(32):
                    pe_t0 = b.run("tensor", lambda e, k=k: e.transpose(
                        psT[:], stag[:, 128 * k:128 * (k + 1)], idn), inc="pe")
                    b.wait("vector", "pe", pe_t0)
                    ve_t0 = b.run("vector", lambda e, k=k: e.tensor_scalar(
                        bblk[:, 128 * k:128 * (k + 1)], psT[:], coefR[:], 2.0,
                        op0=ALU.mult, op1=ALU.mult), inc="ve")
                    b.run("vector", lambda e: e.drain())
                    b.wait("tensor", "ve", ve_t0)

            load_group(0)
            load_group(1)
            work = []
            for g in range(NG):
                for h4 in range(4):
                    for qt in range(NQT):
                        for cg in range(qt + 1):
                            work.append((g, h4, qt, cg))
            qslot_of = {}

            def emit_S(idx):
                g, h4, qt, cg = work[idx]
                pr = idx % 2
                if cg == 0:
                    qs = qts.next()
                    qts.load(qs, lambda e, dd, g=g, h4=h4, qt=qt: e.dma_start(
                        out=dd, in_=Qfm[128 * (4 * g + h4):128 * (4 * g + h4 + 1),
                                        QT * qt:QT * (qt + 1)]))
                    qslot_of[(g, h4, qt)] = qs
                qs = qslot_of[(g, h4, qt)]
                kslot = g % 2
                qts.wait_ready("tensor", qs)
                if cg == 0 and h4 == 0 and qt == 0:
                    kgs.wait_ready("tensor", kslot)
                    vhs.wait_ready("tensor", kslot)
                b.waits("tensor", pairfree[pr])
                pairfree[pr] = []
                for u in range(4):
                    c = 4 * cg + u
                    fn = lambda e, pr=pr, u=u, c=c, kslot=kslot, qs=qs: e.matmul(
                        s_sub(pr, u),
                        kg[:, kslot * TT_ + 128 * c:kslot * TT_ + 128 * (c + 1)],
                        qtl[:, qs * QT:(qs + 1) * QT],
                        start=(u % 2 == 0), stop=(u % 2 == 1))
                    if u < 3:
                        b.run("tensor", fn)
                    else:
                        pe_t = b.run("tensor", fn, inc="pe")
                if cg == qt:
                    qts.consumed(qs, "pe", pe_t)
                pslot = psl.next()
                b.wait("scalar", "pe", pe_t)
                b.waits("scalar", psl.free[pslot])
                psl.free[pslot] = []
                ac_t = b.run("scalar", lambda e, pr=pr, pslot=pslot: e.activation(
                    pbf[:, pslot * 1024:(pslot + 1) * 1024], s_pair(pr), AF.Exp,
                    bias=negm_col[:], scale=SCL), inc="ac")
                pairfree[pr] = [("ac", ac_t)]
                conds = [("ac", ac_t)]
                if cg == qt:
                    b.wait("vector", "ac", ac_t)
                    ve_t = b.run("vector", lambda e, pslot=pslot, qt=qt: e.tensor_tensor(
                        pbf[:, pslot * 1024:(pslot + 1) * 1024],
                        pbf[:, pslot * 1024:(pslot + 1) * 1024],
                        maskb[:, 1024 * qt:1024 * (qt + 1)], op=ALU.mult), inc="ve")
                    b.run("vector", lambda e: e.drain())
                    conds = [("ve", ve_t)]
                psl.produce(pslot, conds)

            def emit_PV(idx):
                g, h4, qt, cg = work[idx]
                pr = idx % 3
                obp = ((4 * g + h4) * NQT + qt) % 2
                kslot = g % 2
                psl.wait_ready("tensor", pr)
                if cg == 0:
                    b.waits("tensor", obankfree[obp])
                    obankfree[obp] = []
                for u in range(4):
                    c = 4 * cg + u
                    for sub in range(2):
                        st_fl = (cg == 0 and u == 0 and sub == 0)
                        sp_fl = (cg == qt and u == 3 and sub == 1)
                        fn = lambda e, pr=pr, u=u, sub=sub, c=c, kslot=kslot, \
                             obp=obp, st_fl=st_fl, sp_fl=sp_fl: e.matmul(
                            ps[:, 512 * (4 + obp) + 129 * sub:
                               512 * (4 + obp) + 129 * (sub + 1)],
                            pbf[:, pr * 1024 + 256 * u + 128 * sub:
                                pr * 1024 + 256 * u + 128 * (sub + 1)],
                            vh[:, kslot * 32 * 129 + 129 * c:
                               kslot * 32 * 129 + 129 * (c + 1)],
                            start=st_fl, stop=sp_fl)
                        if u == 3 and sub == 1:
                            pe_t = b.run("tensor", fn, inc="pe")
                        else:
                            b.run("tensor", fn)
                psl.consumed(pr, "pe", pe_t)
                if cg == qt:
                    os_ = osl.next()
                    b.wait("vector", "pe", pe_t)
                    b.waits("vector", osl.free[os_])
                    osl.free[os_] = []
                    obb = 4 + obp
                    b.run("vector", lambda e, obb=obb: e.reciprocal(
                        recl[:, 0:1], ps[:, 512 * obb + 128:512 * obb + 129]))
                    b.run("vector", lambda e, obb=obb: e.reciprocal(
                        recl[:, 1:2], ps[:, 512 * obb + 257:512 * obb + 258]))
                    b.run("vector", lambda e: e.drain())
                    ovv = oev[:, os_ * QT:(os_ + 1) * QT]
                    b.run("vector", lambda e, obb=obb, ovv=ovv: e.tensor_scalar(
                        ovv[:, 0:128], ps[:, 512 * obb:512 * obb + 128],
                        recl[:, 0:1], None, op0=ALU.mult))
                    ve_t = b.run("vector", lambda e, obb=obb, ovv=ovv: e.tensor_scalar(
                        ovv[:, 128:256], ps[:, 512 * obb + 129:512 * obb + 257],
                        recl[:, 1:2], None, op0=ALU.mult), inc="ve")
                    b.run("vector", lambda e: e.drain())
                    obankfree[obp] = [("ve", ve_t)]
                    b.wait("sync", "ve", ve_t)
                    h = 4 * g + h4
                    osl.store(os_, lambda e, qt=qt, h=h, ovv=ovv: e.dma_start(
                        out=Otm[QT * qt:QT * (qt + 1), 128 * h:128 * (h + 1)]
                        .rearrange("(s p) d -> p s d", p=128),
                        in_=ovv.rearrange("p (s d) -> p s d", s=2)))
                if (h4, qt, cg) == (3, NQT - 1, NQT - 1) and g + 2 < NG:
                    kgs.consumed(kslot, "pe", pe_t)
                    vhs.consumed(kslot, "pe", pe_t)
                    load_group(g + 2)

            for idx in range(len(work)):
                emit_S(idx)
                if idx > 1:
                    emit_PV(idx - 2)
            emit_PV(len(work) - 2)
            emit_PV(len(work) - 1)
            b.wait_group("sync", "s_bo")
        b.barrier("B")

    if "C1" in PHASES:
        with contextlib.ExitStack() as phs:
            wob = phs.enter_context(nc.sbuf_tensor("c1_wo", [128, 2 * 8192], BF16))
            bo_bf = phs.enter_context(nc.sbuf_tensor("c1_bo", [1, DM], BF16))
            b.fence_loads("gpsimd")
            b.dma("gpsimd", lambda e: e.dma_start(out=bo_bf[:], in_=bo), "d_c")
            xrb = phs.enter_context(nc.sbuf_tensor("c1_xr", [128, 4 * 256], F32))
            yeb = phs.enter_context(nc.sbuf_tensor("c1_ye", [128, 4 * 256], F32))
            jnk = phs.enter_context(nc.sbuf_tensor("c1_jk", [128, 256], BF16))
            for c in range(32):
                b.dma("sync", lambda e, c=c: e.dma_start_transpose(
                    bview(2048 * c, 2048), Otm[:, 128 * c:128 * (c + 1)]), "d_t")
            b.wait("tensor", "d_t", b.val("d_t"))
            b.wait("tensor", "d_c", b.val("d_c"))
            wsl = Slots(b, "c1w", 2, lambda s: wob[:, s * 8192:(s + 1) * 8192])
            xsl = Slots(b, "c1x", 4, lambda s: xrb[:, s * 256:(s + 1) * 256])
            ysl = Slots(b, "c1y", 4, lambda s: yeb[:, s * 256:(s + 1) * 256])
            bankfree = {0: [], 1: [], 2: []}
            b.fence_loads("gpsimd")
            pe_t = 0

            def load_x(ow, tt):
                xs = xsl.next()
                xsl.load(xs, lambda e, dd, ow=ow, tt=tt: e.dma_start(
                    out=dd, in_=xq[128 * tt:128 * (tt + 1),
                                   256 * ow:256 * (ow + 1)]))
                return xs

            xq_queue = []
            for ow in range(16):
                s = ow % 2
                wsl.load(s, lambda e, dd, ow=ow: e.dma_start(
                    out=dd.rearrange("p (k o) -> p k o", o=256),
                    in_=wo.rearrange("(k p) o -> p k o", p=128)
                    [:, :, 256 * ow:256 * (ow + 1)]), queue="gpsimd")
                wsl.wait_ready("tensor", s)
                for tt in range(16):
                    bk = (16 * ow + tt) % 3
                    while len(xq_queue) < 3 and (16 * ow + tt) + len(xq_queue) < 256:
                        nn = (16 * ow + tt) + len(xq_queue)
                        xq_queue.append(load_x(nn // 16, nn % 16))
                    xs = xq_queue.pop(0)
                    b.waits("tensor", bankfree[bk])
                    bankfree[bk] = []
                    for k in range(32):
                        b.run("tensor", lambda e, s=s, k=k, tt=tt, bk=bk: e.matmul(
                            bank(bk, 256), bview(2048 * k + 128 * tt, 128),
                            wob[:, s * 8192 + 256 * k:s * 8192 + 256 * (k + 1)],
                            start=(k == 0), stop=False))
                    pe_t = b.run("tensor", lambda e, bk=bk, ow=ow: e.matmul(
                        bank(bk, 256), ones1[:], bo_bf[0:1, 256 * ow:256 * (ow + 1)],
                        start=False, stop=True), inc="pe")
                    ys = ysl.next()
                    b.wait("vector", "pe", pe_t)
                    xsl.wait_ready("vector", xs)
                    b.waits("vector", ysl.free[ys])
                    ysl.free[ys] = []
                    yv = yeb[:, ys * 256:(ys + 1) * 256]
                    xv = xrb[:, xs * 256:(xs + 1) * 256]
                    ve_t = b.run("vector", lambda e, bk=bk, yv=yv, xv=xv:
                                 e.tensor_tensor(yv, bank(bk, 256), xv, op=ALU.add),
                                 inc="ve")
                    bankfree[bk] = [("ve", ve_t)]
                    xsl.consumed(xs, "ve", ve_t)
                    b.run("vector", lambda e: e.drain())
                    col = 16 * tt + ow
                    ve_t = b.run("vector", lambda e, yv=yv, col=col: e.reduce_sum(
                        s1all[:, col:col + 1], yv, axis=AX), inc="ve")
                    b.wait("scalar", "ve", ve_t)
                    ac_t = b.run("scalar", lambda e, yv=yv, col=col: e.activation(
                        jnk[:], yv, AF.Square, accum_out=s2all[:, col:col + 1]),
                        inc="ac")
                    b.wait("sync", "ve", ve_t)
                    ysl.store(ys, lambda e, tt=tt, ow=ow, yv=yv: e.dma_start(
                        out=Ysc[128 * tt:128 * (tt + 1), 256 * ow:256 * (ow + 1)],
                        in_=yv))
                    ysl.consumed(ys, "ac", ac_t)
                wsl.consumed(s, "pe", pe_t)
            b.wait("vector", "ac", b.val("ac"))
            b.run("vector", lambda e: e.drain())
            for tt in range(16):
                b.run("vector", lambda e, tt=tt: e.reduce_sum(
                    ln2mu[:, tt:tt + 1], s1all[:, 16 * tt:16 * (tt + 1)], axis=AX))
            for tt in range(16):
                b.run("vector", lambda e, tt=tt: e.reduce_sum(
                    ln2rs[:, tt:tt + 1], s2all[:, 16 * tt:16 * (tt + 1)], axis=AX))
            b.run("vector", lambda e: e.drain())
            b.run("vector", lambda e: e.tensor_scalar(
                ln2mu[:], ln2mu[:], 1.0 / DM, None, op0=ALU.mult))
            b.run("vector", lambda e: e.drain())
            b.run("vector", lambda e: e.tensor_tensor(
                statc[:], ln2mu[:], ln2mu[:], op=ALU.mult))
            b.run("vector", lambda e: e.drain())
            b.run("vector", lambda e: e.tensor_scalar(
                ln2rs[:], ln2rs[:], 1.0 / DM, None, op0=ALU.mult))
            b.run("vector", lambda e: e.drain())
            ve_t = b.run("vector", lambda e: e.tensor_tensor(
                ln2rs[:], ln2rs[:], statc[:], op=ALU.subtract), inc="ve")
            b.wait("scalar", "ve", ve_t)
            ac_t = b.run("scalar", lambda e: e.activation(
                ln2rs[:], ln2rs[:], AF.Sqrt, bias=eps_col[:]), inc="ac")
            b.wait("vector", "ac", ac_t)
            b.run("vector", lambda e: e.reciprocal(ln2rs[:], ln2rs[:]))
            b.run("vector", lambda e: e.drain())
            b.wait_group("sync", "s_c1y")
        b.barrier("C1")

    if "C2" in PHASES:
        with contextlib.ExitStack() as phs:
          with contextlib.ExitStack() as ph1:
            ytb = ph1.enter_context(nc.sbuf_tensor("c2_yt", [128, 2 * DM], BF16))
            ynb = ph1.enter_context(nc.sbuf_tensor("c2_yn", [128, 2 * DM], BF16))
            ysl = Slots(b, "c2y", 2, lambda s: ytb[:, s * DM:(s + 1) * DM])
            nsl = Slots(b, "c2n", 2, lambda s: ynb[:, s * DM:(s + 1) * DM])
            b.fence_loads("gpsimd")

            def load_yt(tt):
                ss = ysl.next()
                ysl.load(ss, lambda e, dd, tt=tt: e.dma_start(
                    out=dd, in_=Ysc[128 * tt:128 * (tt + 1), :]), queue="gpsimd")
                return ss

            yslot = {0: load_yt(0)}
            for tt in range(16):
                s = yslot.pop(tt)
                if tt + 1 < 16:
                    yslot[tt + 1] = load_yt(tt + 1)
                ns = nsl.next()
                ysl.wait_ready("vector", s)
                b.waits("vector", nsl.free[ns])
                nsl.free[ns] = []
                yv = ytb[:, s * DM:(s + 1) * DM]
                nv = ynb[:, ns * DM:(ns + 1) * DM]
                ve_t = b.run("vector", lambda e, yv=yv, nv=nv, tt=tt: e.tensor_scalar(
                    nv, yv, ln2mu[:, tt:tt + 1], ln2rs[:, tt:tt + 1],
                    op0=ALU.subtract, op1=ALU.mult), inc="ve")
                b.run("vector", lambda e: e.drain())
                ysl.consumed(s, "ve", ve_t)
                b.wait("sync", "ve", ve_t)
                nsl.store(ns, lambda e, tt=tt, nv=nv: e.dma_start(
                    out=YNtm[128 * tt:128 * (tt + 1), :], in_=nv))

          if True:
            b.wait_group("sync", "s_c2n")
            for c in range(32):
                b.dma("sync", lambda e, c=c: e.dma_start_transpose(
                    bview(2048 * c, 2048), YNtm[:, 128 * c:128 * (c + 1)]), "d_t")
            b.wait("tensor", "d_t", b.val("d_t"))
            b.wait("tensor", "d_c", b.val("d_c"))

            for k in range(32):
                for tw in range(4):
                    fn = lambda e, k=k, tw=tw: e.matmul(
                        bank(tw), afm[:, 128 * k:128 * (k + 1)],
                        bview(2048 * k + 512 * tw, 512),
                        start=(k == 0), stop=(k == 31))
                    if k == 31 and tw == 3:
                        pe_t = b.run("tensor", fn, inc="pe")
                    else:
                        b.run("tensor", fn)
            b.wait("vector", "pe", pe_t)
            for tw in range(4):
                b.run("vector", lambda e, tw=tw: e.tensor_copy(
                    h_fm[:, 512 * tw:512 * (tw + 1)], bank(tw)))
            ve_t = b.run("vector", lambda e: e.drain(), inc="ve")
            b.wait("tensor", "ve", ve_t)

            wtb = phs.enter_context(nc.sbuf_tensor("c2_wt", [128, 2 * 8192], BF16,
                                                   is_transpose_dest=True))
            fb_bf = phs.enter_context(nc.sbuf_tensor("c2_fb", [1, DM], BF16))
            b.dma("gpsimd", lambda e: e.dma_start(out=fb_bf[:], in_=fb), "d_c")
            b.wait("tensor", "d_c", b.val("d_c"))
            yrb = phs.enter_context(nc.sbuf_tensor("c2_yr", [128, 4 * 256], F32))
            ob_ = phs.enter_context(nc.sbuf_tensor("c2_ob", [128, 4 * 256], F32))
            wsl = Slots(b, "c2w", 2, lambda s: wtb[:, s * 8192:(s + 1) * 8192])
            yrs = Slots(b, "c2r", 4, lambda s: yrb[:, s * 256:(s + 1) * 256])
            obs = Slots(b, "c2o", 4, lambda s: ob_[:, s * 256:(s + 1) * 256])
            b.wait("sync", "d_wt", b.val("d_wt"))
            b.fence_loads("sync")
            bankfree = {0: [], 1: [], 2: []}

            def load_wt(ow):
                s = ow % 2
                b.waits("sync", wsl.free[s])
                wsl.free[s] = []
                tgt = None
                for k in range(32):
                    tgt = b.dma("sync", lambda e, ow=ow, k=k, s=s: e.dma_start_transpose(
                        wtb[:, s * 8192 + 256 * k:s * 8192 + 256 * (k + 1)],
                        Wt[256 * ow:256 * (ow + 1), 128 * k:128 * (k + 1)]),
                        f"d_wl{s}")
                wsl.ready[s] = [(f"d_wl{s}", tgt)]

            load_wt(0)
            pe_t = 0

            def load_y2(ow, tt):
                ys = yrs.next()
                yrs.load(ys, lambda e, dd, ow=ow, tt=tt: e.dma_start(
                    out=dd, in_=Ysc[128 * tt:128 * (tt + 1),
                                    256 * ow:256 * (ow + 1)]))
                return ys

            y_queue = []
            for ow in range(16):
                s = ow % 2
                if ow + 1 < 16:
                    load_wt(ow + 1)
                wsl.wait_ready("tensor", s)
                for tt in range(16):
                    bk = 4 + (16 * ow + tt) % 3
                    while len(y_queue) < 3 and (16 * ow + tt) + len(y_queue) < 256:
                        nn = (16 * ow + tt) + len(y_queue)
                        y_queue.append(load_y2(nn // 16, nn % 16))
                    ys = y_queue.pop(0)
                    b.waits("tensor", bankfree[bk - 4])
                    bankfree[bk - 4] = []
                    for k in range(32):
                        b.run("tensor", lambda e, s=s, k=k, tt=tt, bk=bk: e.matmul(
                            bank(bk, 256), bview(2048 * k + 128 * tt, 128),
                            wtb[:, s * 8192 + 256 * k:s * 8192 + 256 * (k + 1)],
                            start=(k == 0), stop=False))
                    b.run("tensor", lambda e, bk=bk, tt=tt, ow=ow: e.matmul(
                        bank(bk, 256), h_fm[:, 128 * tt:128 * (tt + 1)],
                        bblk[:, 256 * ow:256 * (ow + 1)], start=False, stop=False))
                    pe_t = b.run("tensor", lambda e, bk=bk, ow=ow: e.matmul(
                        bank(bk, 256), ones1[:], fb_bf[0:1, 256 * ow:256 * (ow + 1)],
                        start=False, stop=True), inc="pe")
                    os_ = obs.next()
                    b.wait("vector", "pe", pe_t)
                    yrs.wait_ready("vector", ys)
                    b.waits("vector", obs.free[os_])
                    obs.free[os_] = []
                    ovv = ob_[:, os_ * 256:(os_ + 1) * 256]
                    yvv = yrb[:, ys * 256:(ys + 1) * 256]
                    ve_t = b.run("vector", lambda e, bk=bk, ovv=ovv, yvv=yvv:
                                 e.tensor_tensor(ovv, bank(bk, 256), yvv, op=ALU.add),
                                 inc="ve")
                    b.run("vector", lambda e: e.drain())
                    bankfree[bk - 4] = [("ve", ve_t)]
                    yrs.consumed(ys, "ve", ve_t)
                    b.wait("sync", "ve", ve_t)
                    obs.store(os_, lambda e, tt=tt, ow=ow, ovv=ovv: e.dma_start(
                        out=out[128 * tt:128 * (tt + 1), 256 * ow:256 * (ow + 1)],
                        in_=ovv))
                wsl.consumed(s, "pe", pe_t)
            b.wait_group("sync", "s_c2o")

    taps = []
    if "xn" in DEBUG_TAPS:
        taps += [("t_xnkv", xn_kv), ("t_xnq", xn_q)]
    if "kv" in DEBUG_TAPS:
        taps += [("t_kfm", Kfm), ("t_vtm", Vtm)]
    if "q" in DEBUG_TAPS:
        taps += [("t_qfm", Qfm)]
    if "o" in DEBUG_TAPS:
        taps += [("t_otm", Otm)]
    if "y" in DEBUG_TAPS:
        taps += [("t_ysc", Ysc)]
    for name, src in taps:
        dstp = nc.dram_tensor(name, list(src.shape), src.dtype,
                              kind="ExternalOutput").ap()
        b.dma("sync", lambda e, dstp=dstp, src=src: e.dma_start(out=dstp, in_=src),
              "d_tap")
    if taps:
        b.wait("sync", "d_tap", b.val("d_tap"))

    with nc.Block() as block:
        @block.sync
        def _(e):
            for f in b.ops["sync"]:
                f(e)
        @block.tensor
        def _(e):
            for f in b.ops["tensor"]:
                f(e)
        @block.vector
        def _(e):
            for f in b.ops["vector"]:
                f(e)
        @block.scalar
        def _(e):
            for f in b.ops["scalar"]:
                f(e)
        @block.gpsimd
        def _(e):
            for f in b.ops["gpsimd"]:
                f(e)
    return nc


def prepare_inputs(inputs):
    x = np.asarray(inputs["L0_attn_x"], np.float32)
    Wq = np.asarray(inputs["L0_attn_Wq"], np.float32)
    Wk = np.asarray(inputs["L0_attn_Wk"], np.float32)
    Wv = np.asarray(inputs["L0_attn_Wv"], np.float32)
    Wo = np.asarray(inputs["L0_attn_Wo"], np.float32)
    bo = np.asarray(inputs["L0_attn_bo"], np.float32).reshape(1, DM)
    lora_A = np.asarray(inputs["L0_ffn_lora_A"], np.float32)
    lora_B = np.asarray(inputs["L0_ffn_lora_B"], np.float32)
    W = np.asarray(inputs["L0_ffn_W"], np.float32)
    fbias = np.asarray(inputs["L0_ffn_bias"], np.float32).reshape(1, DM)
    lwt = np.asarray(inputs["L0_ffn_lora_weights"], np.float32)

    pv = _perm_vec()
    qcols = np.concatenate([128 * h + pv for h in range(NH)])
    kcols = np.concatenate([128 * g + pv for g in range(NG)])
    wq_p = np.ascontiguousarray(Wq[:, qcols])
    wk_p = np.ascontiguousarray(Wk[:, kcols])
    wv_c = np.ascontiguousarray(Wv)
    lA_ = np.ascontiguousarray(lora_A.reshape(128, DM))
    lB_ = np.ascontiguousarray(lora_B.transpose(1, 0, 2).reshape(DM, 128))
    lw_ = np.ascontiguousarray(np.repeat(lwt, 16).reshape(128, 1))
    tka, tkb = _trig(np.arange(TT_))
    iden = np.eye(128, dtype=np.float32)
    swpm = np.roll(iden, 64, axis=0).astype(np.float32)

    in_maps = []
    for c in range(8):
        bb, j = c // 2, c % 2
        rows = np.concatenate([np.arange(QT * (2 * i + j), QT * (2 * i + j + 1))
                               for i in range(NQT)])
        tqa_, tqb_ = _trig(rows)
        in_maps.append(dict(
            xkv=np.ascontiguousarray(x[bb]),
            xq=np.ascontiguousarray(x[bb][rows]),
            wq=wq_p, wk=wk_p, wv=wv_c, wo=np.ascontiguousarray(Wo), bo=bo,
            wf=np.ascontiguousarray(W), fb=fbias, lA=lA_, lB=lB_, lw=lw_,
            tqa=np.ascontiguousarray(tqa_), tqb=np.ascontiguousarray(tqb_),
            tka=np.ascontiguousarray(tka), tkb=np.ascontiguousarray(tkb),
            msk=_masks(j), iden=iden, swp=swpm,
        ))
    return in_maps


_NC_CACHE = {}


def kernel(**inputs):
    in_maps = prepare_inputs(inputs)
    key = (tuple(PHASES), tuple(DEBUG_TAPS))
    if key not in _NC_CACHE:
        _NC_CACHE[key] = build_nc()
    nc = _NC_CACHE[key]
    res = run_bass_kernel_spmd(nc, in_maps, core_ids=list(range(8)))
    full = np.empty((BT, TT_, DM), np.float32)
    for c in range(8):
        bb, j = c // 2, c % 2
        o = res.results[c]["out"]
        for i in range(NQT):
            full[bb, QT * (2 * i + j):QT * (2 * i + j + 1)] = o[QT * i:QT * (i + 1)]
    return full
